# version 03
# baseline (speedup 1.0000x reference)
import numpy as np
import ml_dtypes
from contextlib import ExitStack

import concourse.bass as bass
import concourse.mybir as mybir
from concourse.bass_utils import run_bass_kernel_spmd

F32 = mybir.dt.float32
BF16 = mybir.dt.bfloat16
AF = mybir.ActivationFunctionType
ALU = mybir.AluOpType
AX = mybir.AxisListType

D = 1024
KC = 8
DFF = 4096
NEXP = 8
TOK = 4096
NSLOT = 8
SLOT = 512
NGRP = 2
GRP = 2048
NTT = 4
TT = 512
NTS = 16
NPIECE = 8
HCP = 4
NEG = -30000.0
LN_EPS = 1e-5

STAGE = 3
SKIPSTEPS = False
DBG_NO_MASKS = False
DBG_NO_QX1 = False
DBG_NO_WO = False
DBG_NO_XVDUMP = False


class Ctr:
    __slots__ = ("h", "n", "name")

    def __init__(self, h, name):
        self.h = h
        self.n = 0
        self.name = name


class St:
    __slots__ = ("ops", "name", "seen", "ctr", "inck", "skip_self")

    def __init__(self, name, ctr, inck=1, skip_self=True):
        self.ops = []
        self.name = name
        self.seen = {}
        self.ctr = ctr
        self.inck = inck
        self.skip_self = skip_self

    def op(self, fn, waits=(), sw=False):
        ws = []
        if sw:
            ws.append((self.ctr, self.ctr.n))
        for c, v in waits:
            if c is None:
                continue
            if self.skip_self and c is self.ctr:
                continue
            assert v <= c.n, (
                f"[{self.name}] forward wait on {c.name}: need {v} > planned {c.n}"
            )
            if self.seen.get(c.name, -1) < v:
                ws.append((c, v))
                self.seen[c.name] = v
        self.ctr.n += self.inck

        def f(eng, fn=fn, ws=tuple(ws), sem=self.ctr.h, inck=self.inck):
            for c, v in ws:
                eng.wait_ge(c.h, v)
            ins = fn(eng)
            ins.then_inc(sem, inck)

        self.ops.append(f)
        return self.ctr.n


def build_nc(stage=STAGE):
    nc = bass.Bass("TRN2", target_bir_lowering=False, debug=False)

    x_d = nc.dram_tensor("x", [TOK, D], F32, kind="ExternalInput").ap()
    moff_d = nc.dram_tensor("moff", [128, 4], F32, kind="ExternalInput").ap()
    wq_d = nc.dram_tensor("wq", [128, KC * D], F32, kind="ExternalInput").ap()
    wk_d = nc.dram_tensor("wk", [128, KC * D], F32, kind="ExternalInput").ap()
    wv_d = nc.dram_tensor("wv", [128, KC * D], F32, kind="ExternalInput").ap()
    wo_d = nc.dram_tensor("wo", [128, KC * D], F32, kind="ExternalInput").ap()
    wr_d = nc.dram_tensor("wr", [128, KC * 7], F32, kind="ExternalInput").ap()
    wg_d = nc.dram_tensor("wg", [NEXP, NPIECE, 2, 128, 2048], F32, kind="ExternalInput").ap()
    wu_d = nc.dram_tensor("wu", [NEXP, NPIECE, 2, 128, 2048], F32, kind="ExternalInput").ap()
    wd_d = nc.dram_tensor("wd", [NEXP, NPIECE, 2, 128, 2048], F32, kind="ExternalInput").ap()
    out_d = nc.dram_tensor("out", [TOK, D], F32, kind="ExternalOutput").ap()

    x1sp = nc.dram_tensor("x1sp", [TOK, D], F32).ap()
    x2sp = nc.dram_tensor("x2sp", [TOK, D], F32).ap()
    qsp = nc.dram_tensor("qsp", [D, TOK], BF16).ap()
    kb_t = [nc.dram_tensor(f"kb{g}", [D, GRP], BF16) for g in range(NGRP)]
    vb_t = [nc.dram_tensor(f"vb{g}", [GRP, D], BF16) for g in range(NGRP)]
    kg_t = [nc.dram_tensor(f"kg{g}", [4, D, GRP], BF16) for g in range(NGRP)]
    vg_t = [nc.dram_tensor(f"vg{g}", [4, GRP, D], BF16) for g in range(NGRP)]
    db_t = nc.dram_tensor("dbr", [128, 128], BF16)
    dg_t = nc.dram_tensor("dgr", [4, 128, 128], BF16)
    kg = [t.ap() for t in kg_t]
    vg = [t.ap() for t in vg_t]

    with ExitStack() as ctx:
        def sb(name, shape, dt):
            return ctx.enter_context(nc.sbuf_tensor(name, shape, dt))

        def mksem(name):
            return Ctr(ctx.enter_context(nc.semaphore(name)), name)

        identb = sb("identb", [128, 128], BF16)
        identf = sb("identf", [128, 128], F32)
        wrsb = sb("wrsb", [128, KC * 7], F32)
        moffsb = sb("moffsb", [128, 4], F32)
        qscale = sb("qscale", [128, 1], F32)
        vdbg = sb("vdbg", [128, 4096], BF16) if stage == 4 else None
        AFa = sb("AFa", [128, 19840], F32)
        ABa = sb("ABa", [128, 45056], BF16)
        psf = [ctx.enter_context(nc.psum_tensor(f"psf{i}", [128, 512], F32))
               for i in range(8)]

        c_pe = mksem("c_pe")
        c_v = mksem("c_v")
        c_s = mksem("c_s")
        c_sy = mksem("c_sy")
        c_gp = mksem("c_gp")
        c_gpc = mksem("c_gpc")

        PE = St("pe", c_pe)
        VE = St("v", c_v)
        SE = St("s", c_s)
        SY = St("sy", c_sy, inck=16, skip_self=False)
        GPd = St("gp", c_gp, inck=16, skip_self=False)
        GPc = St("gpc", c_gpc)
        gp_ops = []
        GPd.ops = gp_ops
        GPc.ops = gp_ops
        GPc.seen = GPd.seen

        ACC = 0
        XSQ = 16384
        X32 = 17408
        X32T = 18432
        STAT = 19456
        LOGI = 19584
        CMB = 19696
        CTX = 0
        AX1 = 4096
        AX2 = 8192
        SSB = 12288
        IOTA = 13312
        ASM = 15360
        A_NM, A_L, A_NMN, A_DAL, A_ALP, A_ASUM, A_RL = (ASM + i * 4 for i in range(7))
        PBF = [15424, 15936, 16448, 16960]

        XLN = 0
        PB = 16384
        RING = [PB + i * 6144 for i in range(3)]
        HBUF = PB + 18432
        GSB = PB + 26624
        USB = PB + 27648
        XBF_FFN = PB + 26624
        WSL = [PB + 0, PB + 8192]
        KST = [PB + 16384, PB + 18432]
        VST = [PB + 20480, PB + 22528]
        XBF_QKV = PB + 24576
        QSB = 0
        ARING = [4096, 12288]
        MSK = 20480
        WOT = 28672
        PTB = [36864, 37376, 37888, 38400]
        CTXB = 38912
        CTXT = [43008, 44032]

        AFp = AFa.ap()
        ABp = ABa.ap()

        def af(o, n):
            return AFp[:, o:o + n]

        def ab(o, n):
            return ABp[:, o:o + n]

        bank_mark = {}
        state = {}

        def pe_group(fn, banks, waits=()):
            w = list(waits)
            for b in banks:
                m = bank_mark.get(b)
                if m:
                    w.append(m)
            return PE.op(fn, waits=w)

        def consume(banks, ctr, val):
            for b in banks:
                bank_mark[b] = (ctr, val)

        def barrier_wait():
            return [(c_pe, c_pe.n), (c_v, c_v.n), (c_s, c_s.n),
                    (c_sy, c_sy.n), (c_gp, c_gp.n), (c_gpc, c_gpc.n)]

        SY.op(lambda e: e.dma_start(out=moffsb[:, :], in_=moff_d))
        GPd.op(lambda e: e.dma_start(out=wrsb[:, :], in_=wr_d))
        GPc.op(lambda e: e.memset(identb[:, :], 0.0))
        GPc.op(lambda e: e.affine_select(
            out=identb[:, :], in_=identb[:, :], compare_op=ALU.not_equal,
            fill=1.0, base=0, pattern=[[-1, 128]], channel_multiplier=1))
        GPc.op(lambda e: e.memset(identf[:, :], 0.0))
        GPc.op(lambda e: e.affine_select(
            out=identf[:, :], in_=identf[:, :], compare_op=ALU.not_equal,
            fill=1.0, base=0, pattern=[[-1, 128]], channel_multiplier=1))
        GPc.op(lambda e: e.memset(qscale[:, :], 1.0 / 32.0))
        init_gpc = (c_gpc, c_gpc.n)
        init_sy = (c_sy, c_sy.n)

        def ln_group(site, grp_waits):
            router = site != "qkv"
            xbf_base = XBF_QKV if site == "qkv" else XBF_FFN
            ssum = af(STAT + 0, 16)
            ssq = af(STAT + 16, 16)
            rstd = af(STAT + 32, 16)
            nb = af(STAT + 48, 16)
            t1s = af(STAT + 64, 16)
            t2s = af(STAT + 80, 16)

            for ts in range(NTS):
                acct = af(ACC + ts * D, D)
                VE.op(lambda e, a=acct, o=af(STAT + ts, 1): e.reduce_sum(o, a, AX.X),
                      waits=grp_waits if ts == 0 else ())
                SE.op(lambda e, a=acct, o=af(STAT + 16 + ts, 1): e.activation(
                    af(XSQ, D), a, AF.Square, accum_out=o),
                    waits=grp_waits if ts == 0 else ())
            sqr_done = (c_s, c_s.n)
            sd_t = af(STAT + 96, 16)
            VE.op(lambda e: e.tensor_scalar_mul(t1s, ssum, 1.0 / D),
                  waits=[sqr_done], sw=True)
            VE.op(lambda e: e.tensor_mul(t2s, t1s, t1s), sw=True)
            VE.op(lambda e: e.scalar_tensor_tensor(t2s, ssq, 1.0 / D, t2s,
                                                   ALU.mult, ALU.subtract), sw=True)
            vv = VE.op(lambda e: e.tensor_scalar_add(t2s, t2s, LN_EPS), sw=True)
            SE.op(lambda e: e.activation(sd_t, t2s, AF.Sqrt),
                  waits=[(c_v, vv)])
            sv = c_s.n
            VE.op(lambda e: e.reciprocal(rstd, sd_t), waits=[(c_s, sv)])
            VE.op(lambda e: e.scalar_tensor_tensor(nb, t1s, -1.0, rstd,
                                                   ALU.mult, ALU.mult), sw=True)
            vstat = c_v.n

            for ts in range(NTS):
                acct = af(ACC + ts * D, D)
                par = ts % 2
                x32 = af(XSQ if par else X32, D)
                w = [(c_v, vstat), sqr_done]
                pk = state.get(("x32", par))
                if pk:
                    w.append(pk)
                SE.op(lambda e, a=acct, o=x32, ts=ts: e.activation(
                    o, a, AF.Identity, bias=nb[:, ts:ts + 1],
                    scale=rstd[:, ts:ts + 1]), waits=w)
                sB = c_s.n
                fT = (0, 1) if par == 0 else (2, 3)

                def f32T(e, x32=x32, fT=fT):
                    for fc in range(KC):
                        ins = e.transpose(
                            psf[fT[fc // 4]][:, (fc % 4) * 128:(fc % 4) * 128 + 128],
                            x32[:, fc * 128:fc * 128 + 128], identf[:, :])
                    return ins
                peF = pe_group(f32T, fT, waits=[(c_s, sB), init_gpc])
                state[("x32", par)] = (c_pe, peF)
                xo = ABp[:, XLN:XLN + KC * GRP].rearrange(
                    "p (fc t) -> p fc t", fc=KC)[:, :, ts * 128:(ts + 1) * 128]

                def ev(e, fT=fT, xo=xo):
                    e.tensor_copy(xo[:, 0:4, :],
                                  psf[fT[0]].ap().rearrange("p (a b) -> p a b", a=4))
                    return e.tensor_copy(xo[:, 4:8, :],
                                         psf[fT[1]].ap().rearrange("p (a b) -> p a b", a=4))
                evm = VE.op(ev, waits=[(c_pe, peF)])
                markers = [(c_v, evm)]

                if router:
                    x32t = af(X32T, D)
                    w = []
                    pk = state.get("x32t")
                    if pk:
                        w.append(pk)

                    def evf(e, fT=fT, x32t=x32t):
                        e.tensor_copy(x32t[:, 0:512], psf[fT[0]].ap())
                        return e.tensor_copy(x32t[:, 512:1024], psf[fT[1]].ap())
                    evfm = VE.op(evf, waits=w)
                    markers.append((c_v, evfm))
                    lbank = 4 if par == 0 else 5

                    def rmm(e, x32t=x32t, lbank=lbank):
                        for kc in range(KC):
                            ins = e.matmul(psf[lbank][:, 0:7],
                                           x32t[:, kc * 128:kc * 128 + 128],
                                           wrsb[:, kc * 7:kc * 7 + 7],
                                           start=(kc == 0), stop=(kc == KC - 1))
                        return ins
                    peR = pe_group(rmm, (lbank,), waits=[(c_v, evfm)])
                    state["x32t"] = (c_pe, peR)
                    lgt = af(LOGI + ts * 7, 7)
                    evl = VE.op(lambda e, lgt=lgt, lbank=lbank:
                                e.tensor_copy(lgt, psf[lbank][:, 0:7]),
                                waits=[(c_pe, peR)])
                    consume((lbank,), c_v, evl)
                consume(fT, *markers[-1])

            if router:
                e_t = af(X32 + 0, 7)
                em_t = af(X32 + 8, 7)
                tt1 = af(X32 + 16, 7)
                tt2 = af(X32 + 24, 7)
                m1 = af(X32 + 32, 1)
                e1 = af(X32 + 33, 1)
                e2 = af(X32 + 34, 1)
                sS = af(X32 + 35, 1)
                den = af(X32 + 36, 1)
                rcp = af(X32 + 37, 1)
                wfirst = [state[("x32", 0)], state[("x32", 1)]]
                for ts in range(NTS):
                    lgt = af(LOGI + ts * 7, 7)
                    VE.op(lambda e, lgt=lgt: e.reduce_max(m1, lgt, AX.X, negate=True),
                          waits=wfirst if ts == 0 else ())
                    vm = c_v.n
                    SE.op(lambda e, lgt=lgt: e.activation(e_t, lgt, AF.Exp,
                                                          bias=m1, accum_out=sS),
                          waits=[(c_v, vm)] + (wfirst if ts == 0 else []))
                    sm = c_s.n
                    VE.op(lambda e: e.reduce_max(e1, e_t, AX.X), waits=[(c_s, sm)])
                    VE.op(lambda e: e.tensor_single_scalar(tt1, e_t, e1, ALU.is_ge),
                          sw=True)
                    VE.op(lambda e: e.tensor_scalar(em_t, tt1, -1.0, 1.0,
                                                    ALU.mult, ALU.add), sw=True)
                    VE.op(lambda e: e.tensor_mul(em_t, em_t, e_t), sw=True)
                    VE.op(lambda e: e.reduce_max(e2, em_t, AX.X), sw=True)
                    VE.op(lambda e: e.tensor_single_scalar(tt2, em_t, e2, ALU.is_ge),
                          sw=True)
                    VE.op(lambda e: e.scalar_tensor_tensor(den, sS, 1e-9, e1,
                                                           ALU.mult, ALU.add))
                    VE.op(lambda e: e.tensor_add(den, den, e2), sw=True)
                    VE.op(lambda e: e.reciprocal(rcp, den), sw=True)
                    VE.op(lambda e: e.tensor_add(tt1, tt1, tt2))
                    VE.op(lambda e: e.tensor_mul(tt1, tt1, e_t), sw=True)
                    cmb = af(CMB + ts * 8 + 1, 7)
                    VE.op(lambda e, cmb=cmb: e.tensor_scalar(cmb, tt1, rcp, 0.5,
                                                             ALU.mult, ALU.mult),
                          sw=True)

        def ffn_experts(bw):
            xln_ready = (c_v, c_v.n)
            for e_i in range(NEXP):
                for p_i in range(NPIECE):
                    uds = []
                    for u_i in range(2):
                        rs = state.get("ring_rr", 0)
                        state["ring_rr"] = (rs + 1) % 3
                        slot = RING[rs]
                        w = list(bw)
                        fm = state.get(("ring_free", rs))
                        if fm:
                            w.append(fm)
                        for nm_, dsrc in (("g", wg_d), ("u", wu_d), ("d", wd_d)):
                            off = slot + {"g": 0, "u": 2048, "d": 4096}[nm_]
                            GPd.op(lambda e, d=dsrc[e_i, p_i, u_i], o=ab(off, 2048):
                                   e.dma_start(out=o, in_=d), waits=w)
                            w = []
                        uds.append((rs, c_gp.n))
                    w_h = state.get("hfree")
                    for hc in range(HCP):
                        rs, gval = uds[hc // 2]
                        slot = RING[rs]
                        dl = hc % 2
                        for tt in range(NTT):
                            uc = state.get("ffn_unit", 0)
                            state["ffn_unit"] = uc + 1
                            gb, ub = uc % 2, 2 + uc % 2

                            def gu(e, slot=slot, dl=dl, tt=tt, gb=gb, ub=ub):
                                for kc in range(KC):
                                    e.matmul(psf[gb][:, :],
                                             ab(slot + kc * 256 + dl * 128, 128),
                                             ab(XLN + kc * GRP + tt * TT, TT),
                                             start=(kc == 0), stop=(kc == KC - 1))
                                for kc in range(KC):
                                    ins = e.matmul(psf[ub][:, :],
                                                   ab(slot + 2048 + kc * 256 + dl * 128, 128),
                                                   ab(XLN + kc * GRP + tt * TT, TT),
                                                   start=(kc == 0), stop=(kc == KC - 1))
                                return ins
                            pem = pe_group(gu, (gb, ub),
                                           waits=[(c_gp, gval), xln_ready])
                            gsb = ab(GSB + (uc % 2) * 512, 512)
                            usb = ab(USB + (uc % 2) * 512, 512)
                            w = []
                            pk = state.get(("gub", uc % 2))
                            if pk:
                                w.append(pk)
                            SE.op(lambda e, gb=gb, o=gsb: e.activation(
                                o, psf[gb][:, :], AF.Silu), waits=[(c_pe, pem)] + w)
                            SE.op(lambda e, ub=ub, o=usb: e.copy(o, psf[ub][:, :]))
                            sm = c_s.n
                            consume((gb, ub), c_s, sm)
                            hdst = ab(HBUF + hc * GRP + tt * TT, TT)
                            w = [(c_s, sm)]
                            if w_h and hc == 0 and tt == 0:
                                w.append(w_h)
                            vm = VE.op(lambda e, g=gsb, u=usb, o=hdst:
                                       e.tensor_mul(o, g, u), waits=w)
                            state[("gub", uc % 2)] = (c_v, vm)
                    h_done = (c_v, c_v.n)
                    for ts in range(NTS):
                        pa, pb2 = 4 + 2 * (ts % 2), 5 + 2 * (ts % 2)

                        def dn(e, ts=ts, pa=pa, pb2=pb2, uds=uds):
                            for dffc in range(4):
                                slot = RING[uds[dffc // 2][0]]
                                lhs = ab(HBUF + dffc * GRP + ts * 128, 128)
                                e.matmul(psf[pa][:, :], lhs,
                                         ab(slot + 4096 + (dffc % 2) * 1024, 512),
                                         start=(dffc == 0), stop=(dffc == 3))
                                ins = e.matmul(psf[pb2][:, :], lhs,
                                               ab(slot + 4096 + (dffc % 2) * 1024 + 512, 512),
                                               start=(dffc == 0), stop=(dffc == 3))
                            return ins
                        pem = pe_group(dn, (pa, pb2), waits=[h_done])
                        if e_i == 0:
                            sc = 0.5
                        else:
                            sc = af(CMB + ts * 8 + e_i, 1)
                        VE.op(lambda e, ts=ts, pa=pa, s=sc: e.scalar_tensor_tensor(
                            af(ACC + ts * D, 512), psf[pa][:, :], s,
                            af(ACC + ts * D, 512), ALU.mult, ALU.add),
                            waits=[(c_pe, pem)])
                        vm = VE.op(lambda e, ts=ts, pb2=pb2, s=sc: e.scalar_tensor_tensor(
                            af(ACC + ts * D + 512, 512), psf[pb2][:, :], s,
                            af(ACC + ts * D + 512, 512), ALU.mult, ALU.add))
                        consume((pa, pb2), c_v, vm)
                    pe_end = (c_pe, c_pe.n)
                    for rs, _ in uds:
                        state[("ring_free", rs)] = pe_end
                    state["hfree"] = pe_end

        def qkv_group(gi):
            pe_exp_end = (c_pe, c_pe.n)
            acc_final = state["acc_final"]
            ln_group("qkv", [acc_final])
            gq = GPd.op(lambda e: e.dma_start(out=ab(WSL[0], 8192), in_=wq_d),
                        waits=[pe_exp_end])
            gk = GPd.op(lambda e: e.dma_start(out=ab(WSL[1], 8192), in_=wk_d))
            xln_ready = (c_v, c_v.n)

            def proj_fm(wslot, gw, dst, scale):
                for dc in range(KC):
                    par = dc % 2
                    stg = KST[par]
                    last = None
                    for tt in range(NTT):
                        bk = (dc * NTT + tt) % 2

                        def pj(e, wslot=wslot, dc=dc, tt=tt, bk=bk):
                            for kc in range(KC):
                                ins = e.matmul(psf[bk][:, :],
                                               ab(wslot + kc * D + dc * 128, 128),
                                               ab(XLN + kc * GRP + tt * TT, TT),
                                               start=(kc == 0), stop=(kc == KC - 1))
                            return ins
                        pem = pe_group(pj, (bk,), waits=[(c_gp, gw), xln_ready])
                        dm = state.get(("stg", par))
                        w2 = [(c_pe, pem)] + ([dm] if dm else [])
                        if scale is None:
                            SE.op(lambda e, bk=bk, o=ab(stg + tt * TT, TT):
                                  e.copy(o, psf[bk][:, :]), waits=w2)
                        else:
                            SE.op(lambda e, bk=bk, o=ab(stg + tt * TT, TT):
                                  e.mul(o, psf[bk][:, :], qscale[:, :]),
                                  waits=w2 + [init_gpc])
                        consume((bk,), c_s, c_s.n)
                        last = c_s.n
                    c0 = 0 if dst.shape[1] == GRP else gi * GRP
                    dmam = SY.op(lambda e, stg=stg, dc=dc, dst=dst, c0=c0:
                                 e.dma_start(out=dst[dc * 128:dc * 128 + 128,
                                                     c0:c0 + GRP],
                                             in_=ab(stg, GRP)),
                                 waits=[(c_s, last)])
                    state[("stg", par)] = (c_sy, dmam)

            proj_fm(WSL[0], gq, qsp, 1.0 / 32.0)
            gv = GPd.op(lambda e: e.dma_start(out=ab(WSL[0], 8192), in_=wv_d),
                        waits=[(c_pe, c_pe.n)])
            proj_fm(WSL[1], gk, kb_t[gi].ap(), None)

            for ts in range(NTS):
                par = (ts // 2) % 2
                pa, pb2 = 2 + 2 * (ts % 2), 3 + 2 * (ts % 2)

                def vp(e, ts=ts, pa=pa, pb2=pb2):
                    for kc in range(KC):
                        lhs = ab(XLN + kc * GRP + ts * 128, 128)
                        e.matmul(psf[pa][:, :], lhs, ab(WSL[0] + kc * D, 512),
                                 start=(kc == 0), stop=(kc == KC - 1))
                        ins = e.matmul(psf[pb2][:, :], lhs,
                                       ab(WSL[0] + kc * D + 512, 512),
                                       start=(kc == 0), stop=(kc == KC - 1))
                    return ins
                pem = pe_group(vp, (pa, pb2), waits=[(c_gp, gv), xln_ready])
                stg = VST[par] + (ts % 2) * 1024
                dm = state.get(("stgv", par))
                w2 = [(c_pe, pem)] + ([dm] if dm else [])
                SE.op(lambda e, pa=pa, o=ab(stg, 512): e.copy(o, psf[pa][:, :]),
                      waits=w2)
                SE.op(lambda e, pb2=pb2, o=ab(stg + 512, 512): e.copy(o, psf[pb2][:, :]))
                consume((pa, pb2), c_s, c_s.n)
                if ts % 2 == 1:
                    r0 = (ts - 1) * 128
                    vout = vb_t[gi].ap()[r0:r0 + 256, :].rearrange("(a p) f -> p a f", p=128)
                    vin = ABp[:, VST[par]:VST[par] + 2048].rearrange(
                        "p (a f) -> p a f", a=2)
                    dmam = SY.op(lambda e, vout=vout, vin=vin:
                                 e.dma_start(out=vout, in_=vin),
                                 waits=[(c_s, c_s.n)])
                    state[("stgv", par)] = (c_sy, dmam)

            xo = x1sp[gi * GRP:(gi + 1) * GRP, :].rearrange("(a p) f -> p a f", p=128)
            xi = AFp[:, ACC:ACC + NTS * D].rearrange("p (a f) -> p a f", a=NTS)
            SY.op(lambda e, xo=xo, xi=xi: e.dma_start(out=xo, in_=xi),
                  waits=[acc_final])

        def ffn_call(site, gi, src):
            bw = barrier_wait()
            xi = src[gi * GRP:(gi + 1) * GRP, :].rearrange("(a p) f -> p a f", p=128)
            xo = AFp[:, ACC:ACC + NTS * D].rearrange("p (a f) -> p a f", a=NTS)
            ld = SY.op(lambda e, xi=xi, xo=xo: e.dma_start(out=xo, in_=xi),
                       waits=bw)
            ln_group(site, [(c_sy, ld), init_sy] + bw)
            ffn_experts(bw)
            state["acc_final"] = (c_v, c_v.n)

        if stage == 0:
            bw = barrier_wait()
            xi = x_d[0:GRP, :].rearrange("(a p) f -> p a f", p=128)
            xo = AFp[:, ACC:ACC + NTS * D].rearrange("p (a f) -> p a f", a=NTS)
            ld = SY.op(lambda e, xi=xi, xo=xo: e.dma_start(out=xo, in_=xi), waits=bw)
            ln_group("ffn1", [(c_sy, ld), init_sy] + bw)
            lnv = (c_v, c_v.n)
            lnp = (c_pe, c_pe.n)
            xv = ABp[:, XLN:XLN + KC * GRP].rearrange("p (fc t) -> p fc t", fc=KC)
            for hh in range(2):
                ov = out_d[hh * 1024:(hh + 1) * 1024, :].rearrange(
                    "(fc p) t -> p fc t", p=128)
                GPd.op(lambda e, ov=ov, hh=hh: e.dma_start(
                    out=ov, in_=xv[:, :, hh * 1024:(hh + 1) * 1024]),
                    waits=[lnv, lnp])
            SY.op(lambda e: e.dma_start(out=out_d[2048:2176, 0:128], in_=af(CMB, 128)),
                  waits=[lnv])
            SY.op(lambda e: e.dma_start(out=out_d[2176:2304, 0:112], in_=af(LOGI, 112)),
                  waits=[lnv])
            SY.op(lambda e: e.dma_start(out=out_d[2304:2432, 0:128], in_=af(STAT, 128)),
                  waits=[lnv])

        cc_grp = [None, None]
        for gi in range(NGRP if stage > 0 else 0):
            ffn_call("ffn1", gi, x_d)
            if stage >= 2:
                qkv_group(gi)
                bounce_done = (c_sy, c_sy.n)
                GPc.op(lambda e, gi=gi: e.collective_compute(
                    "AllGather", ALU.bypass,
                    replica_groups=[[0, 1, 2, 3], [4, 5, 6, 7]],
                    ins=[kb_t[gi].ap().opt()], outs=[kg_t[gi].ap().opt()]),
                    waits=[bounce_done])
                GPc.op(lambda e, gi=gi: e.collective_compute(
                    "AllGather", ALU.bypass,
                    replica_groups=[[0, 1, 2, 3], [4, 5, 6, 7]],
                    ins=[vb_t[gi].ap().opt()], outs=[vg_t[gi].ap().opt()]))
                GPc.op(lambda e: e.collective_compute(
                    "AllGather", ALU.bypass,
                    replica_groups=[[0, 1, 2, 3], [4, 5, 6, 7]],
                    ins=[db_t.ap().opt()], outs=[dg_t.ap().opt()]))
                cc_grp[gi] = (c_gpc, c_gpc.n)
            else:
                oo = out_d[gi * GRP:(gi + 1) * GRP, :].rearrange("(a p) f -> p a f", p=128)
                xi = AFp[:, ACC:ACC + NTS * D].rearrange("p (a f) -> p a f", a=NTS)
                SY.op(lambda e, oo=oo, xi=xi: e.dma_start(out=oo, in_=xi),
                      waits=[state["acc_final"]])

        if stage >= 2:
            bw = [(c_pe, c_pe.n), (c_v, c_v.n), (c_s, c_s.n),
                  (c_sy, c_sy.n), (c_gp, c_gp.n)]
            GPc.op(lambda e: e.iota(af(IOTA, 2048), pattern=[[1, 2048]], base=0,
                                    channel_multiplier=-1,
                                    allow_small_or_imprecise_dtypes=True),
                   waits=bw)
            io_m = (c_gpc, c_gpc.n)
            for s_i in range(0 if DBG_NO_MASKS else 4):
                mt = ab(MSK + s_i * 2048, 2048)
                VE.op(lambda e, mt=mt, s_i=s_i: e.tensor_single_scalar(
                    mt, af(IOTA, 2048), moffsb[:, s_i:s_i + 1], ALU.is_gt),
                    waits=[io_m, init_sy] + bw)
                VE.op(lambda e, mt=mt: e.tensor_scalar_mul(mt, mt, NEG), sw=True)
            msk_done = (c_v, c_v.n)
            gwo = None if DBG_NO_WO else GPd.op(lambda e: e.dma_start(out=ab(WOT, 8192), in_=wo_d), waits=bw)

            nslot_eff = 1 if stage == 4 else NSLOT
            for s_i in range(nslot_eff):
                n_units = 1 if stage == 4 else 4 * (s_i + 1)
                qin = qsp.rearrange("(dc p) t -> p dc t", p=128)[
                    :, :, s_i * SLOT:(s_i + 1) * SLOT]
                w = list(bw)
                pm = state.get("qsb_free")
                if pm:
                    w.append(pm)
                qld = None if DBG_NO_QX1 else SY.op(lambda e, qin=qin: e.dma_start(
                    out=ABp[:, QSB:QSB + 4096].rearrange("p (dc t) -> p dc t", dc=KC),
                    in_=qin), waits=w)
                x1in = x1sp[s_i * SLOT:(s_i + 1) * SLOT, :].rearrange(
                    "(a p) f -> p a f", p=128)
                w = list(bw)
                pm = state.get("x1_free")
                if pm:
                    w.append(pm)
                x1ld = None if DBG_NO_QX1 else SY.op(lambda e, x1in=x1in: e.dma_start(
                    out=AFp[:, AX1:AX1 + 4096].rearrange("p (a f) -> p a f", a=4),
                    in_=x1in), waits=w)
                w = list(bw)
                pm = state.get("ctx_free")
                if pm:
                    w.append(pm)
                VE.op(lambda e: e.memset(af(CTX, 4096), 0.0), waits=w)
                w = []
                pm = state.get("nm_free")
                if pm:
                    w.append(pm)
                VE.op(lambda e: e.memset(af(A_NM, 4), 30000.0), waits=w)
                VE.op(lambda e: e.memset(af(A_L, 4), 0.0))
                st_init = (c_v, c_v.n)

                unit_dma = []

                def emit_kv_load(u):
                    rs = u % 2
                    g = u // 16
                    w = [cc_grp[g]] + list(bw)
                    fm = state.get(("aring_free", rs))
                    if fm:
                        w.append(fm)
                    klo = (u // 4) * SLOT - g * GRP
                    kin = kg[g][u % 4, :, klo:klo + SLOT]
                    kin = kin.rearrange("(dc p) t -> p dc t", p=128)
                    SY.op(lambda e, kin=kin, rs=rs: e.dma_start(
                        out=ABp[:, ARING[rs]:ARING[rs] + 4096].rearrange(
                            "p (dc t) -> p dc t", dc=KC), in_=kin), waits=w)
                    for tc in range(4):
                        vin = vg[g][u % 4, klo + tc * 128:klo + tc * 128 + 128, :]
                        dm = SY.op(lambda e, vin=vin, rs=rs, tc=tc: e.dma_start(
                            out=ab(ARING[rs] + 4096 + tc * 1024, 1024), in_=vin))
                    if stage == 4 and u == 0:
                        for tc in range(4):
                            vin = vg[0][0, tc * 128:tc * 128 + 128, :]
                            dm = SY.op(lambda e, vin=vin, tc=tc: e.dma_start(
                                out=vdbg.ap()[:, tc * 1024:(tc + 1) * 1024], in_=vin))
                    unit_dma.append(dm)

                for u in range(min(2, n_units)):
                    emit_kv_load(u)

                nqt = 0 if (stage == 4 and SKIPSTEPS) else 4

                def part2(sc_):
                    step, qt, rs, s_exp, pbuf, _u = sc_
                    alp = af(A_ALP + qt, 1)
                    ptp = 4 + step % 2

                    def ptr(e, p=pbuf, ptp=ptp):
                        for kc in range(4):
                            ins = e.transpose(psf[ptp][:, kc * 128:kc * 128 + 128],
                                              p[:, kc * 128:kc * 128 + 128],
                                              identf[:, :])
                        return ins
                    peT = pe_group(ptr, (ptp,), waits=[(c_s, s_exp)])
                    state[("pbuf", step % 4)] = (c_pe, peT)
                    ptb = ab(PTB[step % 4], 512)
                    w = [(c_pe, peT)]
                    pk = state.get(("ptbf", step % 4))
                    if pk:
                        w.append(pk)
                    SE.op(lambda e, ptp=ptp, o=ptb: e.copy(o, psf[ptp][:, :]),
                          waits=w)
                    s_pt = c_s.n
                    consume((ptp,), c_s, s_pt)

                    def av(e, ptb=ptb, rs=rs):
                        for kc in range(4):
                            e.matmul(psf[6][:, :], ptb[:, kc * 128:kc * 128 + 128],
                                     ab(ARING[rs] + 4096 + kc * D, 512),
                                     start=(kc == 0), stop=(kc == 3))
                            ins = e.matmul(psf[7][:, :],
                                           ptb[:, kc * 128:kc * 128 + 128],
                                           ab(ARING[rs] + 4096 + kc * D + 512, 512),
                                           start=(kc == 0), stop=(kc == 3))
                        return ins
                    peA = pe_group(av, (6, 7), waits=[(c_s, s_pt)])
                    state[("ptbf", step % 4)] = (c_pe, peA)
                    w = [(c_pe, peA), st_init]
                    VE.op(lambda e, qt=qt, alp=alp: e.scalar_tensor_tensor(
                        af(CTX + qt * D, 512), af(CTX + qt * D, 512), alp,
                        psf[6][:, :], ALU.mult, ALU.add), waits=w)
                    vm = VE.op(lambda e, qt=qt, alp=alp: e.scalar_tensor_tensor(
                        af(CTX + qt * D + 512, 512), af(CTX + qt * D + 512, 512),
                        alp, psf[7][:, :], ALU.mult, ALU.add))
                    consume((6, 7), c_v, vm)
                    state[("alw", qt)] = (c_v, vm)

                def unit_done(u):
                    state[("aring_free", u % 2)] = (c_pe, c_pe.n)
                    if u + 2 < n_units:
                        emit_kv_load(u + 2)

                step = 0
                pending = []
                steplist = [(u, qt) for u in range(n_units) for qt in range(nqt)]
                for u, qt in steplist:
                    rs = u % 2
                    diag = u >= 4 * s_i
                    if True:
                        spar = step % 4
                        nmv = af(A_NM + qt, 1)
                        nmn = af(A_NMN + qt, 1)
                        dal = af(A_DAL + qt, 1)
                        nmu = af(A_ASUM + qt, 1)
                        alp = af(A_ALP + qt, 1)
                        lv = af(A_L + qt, 1)

                        def sc(e, qt=qt, rs=rs, spar=spar):
                            for dc in range(KC):
                                ins = e.matmul(
                                    psf[spar][:, :],
                                    ABp[:, QSB + dc * SLOT + qt * 128:
                                        QSB + dc * SLOT + qt * 128 + 128],
                                    ab(ARING[rs] + dc * SLOT, SLOT),
                                    start=(dc == 0), stop=(dc == KC - 1))
                            return ins
                        pem = pe_group(sc, (spar,),
                                       waits=[(c_sy, unit_dma[u]), (c_sy, qld)])
                        if diag:
                            ssb = af(SSB + (step % 2) * 512, 512)
                            mslice = ab(MSK + qt * 2048 + (u - 4 * s_i) * 512, 512)
                            w = [(c_pe, pem), msk_done]
                            pk = state.get(("ssb", step % 2))
                            if pk:
                                w.append(pk)
                            vev = VE.op(lambda e, ssb=ssb, m=mslice, spar=spar:
                                        e.tensor_tensor(ssb, psf[spar][:, :], m, ALU.add),
                                        waits=w)
                            consume((spar,), c_v, vev)
                            VE.op(lambda e, ssb=ssb, nmu=nmu:
                                  e.reduce_max(nmu, ssb, AX.X, negate=True), sw=True)
                            exp_src = ssb
                        else:
                            w = [(c_pe, pem), st_init]
                            VE.op(lambda e, spar=spar, nmu=nmu:
                                  e.reduce_max(nmu, psf[spar][:, :], AX.X, negate=True),
                                  waits=w)
                            exp_src = psf[spar][:, :]
                        wnm = state.get(("nmw", qt))
                        VE.op(lambda e, nmn=nmn, nmv=nmv, nmu=nmu:
                              e.tensor_tensor(nmn, nmv, nmu, ALU.min),
                              waits=[wnm] if wnm else [], sw=True)
                        VE.op(lambda e, dal=dal, nmn=nmn, nmv=nmv:
                              e.tensor_sub(dal, nmn, nmv), sw=True)
                        VE.op(lambda e, nmv=nmv, nmn=nmn: e.tensor_copy(nmv, nmn))
                        vnm = c_v.n
                        wal = state.get(("alw", qt))
                        SE.op(lambda e, alp=alp, dal=dal: e.activation(alp, dal, AF.Exp),
                              waits=[(c_v, vnm)] + ([wal] if wal else []))
                        pbuf = af(PBF[step % 4], 512)
                        w = []
                        pk = state.get(("pbuf", step % 4))
                        if pk:
                            w.append(pk)
                        SE.op(lambda e, p=pbuf, src=exp_src, nmn=nmn, nmu=nmu:
                              e.activation(p, src, AF.Exp, bias=nmn, accum_out=nmu),
                              waits=w)
                        s_exp = c_s.n
                        state[("nmw", qt)] = (c_s, s_exp)
                        if diag:
                            state[("ssb", step % 2)] = (c_s, s_exp)
                        else:
                            consume((spar,), c_s, s_exp)
                        VE.op(lambda e, lv=lv, alp=alp, nmu=nmu:
                              e.scalar_tensor_tensor(lv, lv, alp, nmu,
                                                     ALU.mult, ALU.add),
                              waits=[(c_s, s_exp)])

                        cur = (step, qt, rs, s_exp, pbuf, u)
                        pending.append(cur)
                        if len(pending) > 2:
                            pd = pending.pop(0)
                            part2(pd)
                            if pd[1] == nqt - 1:
                                unit_done(pd[5])
                        step += 1
                for pd in pending:
                    part2(pd)
                    if pd[1] == nqt - 1:
                        unit_done(pd[5])
                pending = []
                state["qsb_free"] = (c_pe, c_pe.n)

                if stage == 4:
                    dw = [(c_v, c_v.n), (c_s, c_s.n), (c_pe, c_pe.n),
                          (c_sy, c_sy.n), (c_gp, c_gp.n)]
                    ov = out_d[0:512, :].rearrange("(a p) f -> p a f", p=128)
                    cv2 = AFp[:, CTX:CTX + 4096].rearrange("p (a f) -> p a f", a=4)
                    SY.op(lambda e: e.dma_start(out=ov, in_=cv2), waits=dw)
                    SY.op(lambda e: e.dma_start(out=out_d[512:640, 0:4],
                                                in_=af(A_L, 4)))
                    SY.op(lambda e: e.dma_start(out=out_d[640:768, 0:4],
                                                in_=af(A_NM, 4)))
                    qv = ABp[:, QSB:QSB + 4096].rearrange("p (a f) -> p a f", a=4)
                    GPd.op(lambda e: e.dma_start(
                        out=out_d[768:1280, :].rearrange("(a p) f -> p a f", p=128),
                        in_=qv), waits=dw)
                    GPd.op(lambda e: e.dma_start(out=out_d[1280:1408, :],
                                                 in_=ab(ARING[0], 1024)), waits=dw)
                    vcp = VE.op(lambda e: e.tensor_copy(af(IOTA, 1024),
                                                        ab(ARING[0] + 4096, 1024)),
                                waits=dw)
                    SY.op(lambda e: e.dma_start(out=out_d[1408:1536, :],
                                                in_=af(IOTA, 1024)),
                          waits=[(c_v, vcp)])
                    GPd.op(lambda e: e.dma_start(out=out_d[2304:3328, :],
                                                 in_=vg[0, 0:1024, :]))
                    GPd.op(lambda e: e.dma_start(out=out_d[3328:3456, :],
                                                 in_=ab(MSK, 1024)))
                    GPd.op(lambda e: e.dma_start(out=out_d[3456:3584, :],
                                                 in_=ab(MSK + 1024, 1024)))
                    SY.op(lambda e: e.dma_start(out=out_d[3584:3712, 0:512],
                                                in_=af(PBF[1], 512)), waits=dw)
                    GPd.op(lambda e: e.dma_start(out=out_d[3712:3840, 0:512],
                                                 in_=ab(PTB[1], 512)), waits=dw)

                    GPd.op(lambda e: e.dma_start(out=out_d[3840:3968, :],
                                                 in_=ab(ARING[0] + 4096, 1024)),
                           waits=dw)
                    GPd.op(lambda e: e.dma_start(out=out_d[3968:4096, :],
                                                 in_=ab(ARING[0] + 4096 + 1024, 1024)),
                           waits=dw)
                    SY.op(lambda e: e.dma_start(out=out_d[512:640, 8:12],
                                                in_=af(A_ALP, 4)))
                    SY.op(lambda e: e.dma_start(out=out_d[512:640, 12:16],
                                                in_=af(A_DAL, 4)))
                    GPd.op(lambda e: e.dma_start(out=out_d[1536:1664, :],
                                                 in_=vdbg.ap()[:, 0:1024]), waits=dw)
                    break

                for qt in range(4):
                    rl = af(A_RL + qt, 1)
                    VE.op(lambda e, rl=rl, qt=qt: e.reciprocal(rl, af(A_L + qt, 1)))
                    v_cb = VE.op(lambda e, qt=qt, rl=rl: e.tensor_scalar_mul(
                        af(CTX + qt * D, D), af(CTX + qt * D, D), rl), sw=True)
                    ctp = 4 + qt % 2
                    wpt = state.get(("cttw", qt % 2))
                    for half in range(2):
                        def cT2(e, qt=qt, ctp=ctp, half=half):
                            for fc in range(4):
                                ins = e.transpose(
                                    psf[ctp][:, fc * 128:fc * 128 + 128],
                                    af(CTX + qt * D + (half * 4 + fc) * 128, 128),
                                    identf[:, :])
                            return ins
                        peC = pe_group(cT2, (ctp,), waits=[(c_v, v_cb)])
                        w2 = [(c_pe, peC)]
                        if half == 0 and wpt:
                            w2.append(wpt)
                        SE.op(lambda e, ctp=ctp,
                              o=ab(CTXT[qt % 2] + half * 512, 512):
                              e.copy(o, psf[ctp][:, :]), waits=w2)
                        consume((ctp,), c_s, c_s.n)
                    s_ct = c_s.n

                    def op_(e, qt=qt):
                        for dc in range(KC):
                            lhs = ab(CTXT[qt % 2] + dc * 128, 128)
                            e.matmul(psf[6][:, :], lhs, ab(WOT + dc * D, 512),
                                     start=(dc == 0), stop=(dc == KC - 1))
                            ins = e.matmul(psf[7][:, :], lhs,
                                           ab(WOT + dc * D + 512, 512),
                                           start=(dc == 0), stop=(dc == KC - 1))
                        return ins
                    peO = pe_group(op_, (6, 7), waits=[(c_s, s_ct), (c_gp, gwo)])
                    state[("cttw", qt % 2)] = (c_pe, peO)
                    w = [(c_pe, peO), (c_sy, x1ld)]
                    pm = state.get("x2_free")
                    if pm:
                        w.append(pm)
                    VE.op(lambda e, qt=qt: e.tensor_add(
                        af(AX2 + qt * D, 512), af(AX1 + qt * D, 512), psf[6][:, :]),
                        waits=w)
                    vm = VE.op(lambda e, qt=qt: e.tensor_add(
                        af(AX2 + qt * D + 512, 512), af(AX1 + qt * D + 512, 512),
                        psf[7][:, :]))
                    consume((6, 7), c_v, vm)
                v_x2 = (c_v, c_v.n)
                state["x1_free"] = v_x2
                state["ctx_free"] = (c_pe, c_pe.n)
                state["nm_free"] = (c_s, c_s.n)
                xo2 = x2sp[s_i * SLOT:(s_i + 1) * SLOT, :].rearrange(
                    "(a p) f -> p a f", p=128)
                xi2 = AFp[:, AX2:AX2 + 4096].rearrange("p (a f) -> p a f", a=4)
                dmx = SY.op(lambda e, xo2=xo2, xi2=xi2: e.dma_start(out=xo2, in_=xi2),
                            waits=[v_x2])
                if stage == 2:
                    oo = out_d[s_i * SLOT:(s_i + 1) * SLOT, :].rearrange(
                        "(a p) f -> p a f", p=128)
                    dmx = SY.op(lambda e, oo=oo, xi2=xi2: e.dma_start(out=oo, in_=xi2))
                state["x2_free"] = (c_sy, dmx)

        if stage == 3:
            for gi in range(NGRP):
                ffn_call("ffn2", gi, x2sp)
                oo = out_d[gi * GRP:(gi + 1) * GRP, :].rearrange("(a p) f -> p a f", p=128)
                xi = AFp[:, ACC:ACC + NTS * D].rearrange("p (a f) -> p a f", a=NTS)
                SY.op(lambda e, oo=oo, xi=xi: e.dma_start(out=oo, in_=xi),
                      waits=[state["acc_final"]])

        fin = c_sy.n

        with nc.Block() as block:
            block.tensor(lambda t: ([f(t) for f in PE.ops], None)[1])
            block.vector(lambda v: ([f(v) for f in VE.ops], None)[1])
            block.scalar(lambda s: ([f(s) for f in SE.ops], None)[1])
            block.gpsimd(lambda g: ([f(g) for f in gp_ops], None)[1])

            def syrun(sy):
                for f in SY.ops:
                    f(sy)
                sy.wait_ge(c_sy.h, fin)
            block.sync(syrun)

    return nc


def _relayout_gateup(w):
    t = np.ascontiguousarray(w.T)
    t = t.reshape(8, 128, 8, 2, 256)
    t = t.transpose(2, 3, 1, 0, 4)
    return np.ascontiguousarray(t.reshape(8, 2, 128, 2048))


def _relayout_down(w):
    t = np.ascontiguousarray(w.T)
    t = t.reshape(8, 2, 2, 128, 1024)
    t = t.transpose(0, 1, 3, 2, 4)
    return np.ascontiguousarray(t.reshape(8, 2, 128, 2048))


def _relayout_sq(w):
    t = np.ascontiguousarray(w.T)
    t = t.reshape(8, 128, 1024).transpose(1, 0, 2)
    return np.ascontiguousarray(t.reshape(128, 8192))


_NC_CACHE = {}


def _get_nc(stage=STAGE):
    if stage not in _NC_CACHE:
        _NC_CACHE[stage] = build_nc(stage)
    return _NC_CACHE[stage]


def make_in_maps(inputs):
    f32 = np.float32
    x = np.asarray(inputs["L0_attn_in_features"], dtype=f32)
    wq = _relayout_sq(np.asarray(inputs["L0_attn_w_q"], dtype=f32))
    wk = _relayout_sq(np.asarray(inputs["L0_attn_w_k"], dtype=f32))
    wv = _relayout_sq(np.asarray(inputs["L0_attn_w_v"], dtype=f32))
    wo = _relayout_sq(np.asarray(inputs["L0_attn_w_o"], dtype=f32))
    wr = np.ascontiguousarray(
        np.asarray(inputs["L0_ffn_router_w"], dtype=f32).T.reshape(8, 128, 7)
        .transpose(1, 0, 2).reshape(128, 56))
    sg = np.asarray(inputs["L0_ffn_shared_gate"], dtype=f32)
    su = np.asarray(inputs["L0_ffn_shared_up"], dtype=f32)
    sd = np.asarray(inputs["L0_ffn_shared_down"], dtype=f32)
    rg = np.asarray(inputs["L0_ffn_routed_gate"], dtype=f32)
    ru = np.asarray(inputs["L0_ffn_routed_up"], dtype=f32)
    rd = np.asarray(inputs["L0_ffn_routed_down"], dtype=f32)
    wg = np.stack([_relayout_gateup(sg)] + [_relayout_gateup(rg[i]) for i in range(7)])
    wu = np.stack([_relayout_gateup(su)] + [_relayout_gateup(ru[i]) for i in range(7)])
    wd = np.stack([_relayout_down(sd)] + [_relayout_down(rd[i]) for i in range(7)])

    in_maps = []
    for core in range(8):
        b, c = core // 4, core % 4
        rows = np.concatenate(
            [x[b, 512 * (4 * k + c):512 * (4 * k + c) + 512] for k in range(8)], axis=0)
        moff = np.tile((512 * c + 128 * np.arange(4, dtype=f32))[None, :], (128, 1))
        in_maps.append({
            "x": np.ascontiguousarray(rows), "moff": np.ascontiguousarray(moff),
            "wq": wq, "wk": wk, "wv": wv, "wo": wo, "wr": wr,
            "wg": wg, "wu": wu, "wd": wd,
        })
    return in_maps


def assemble_out(results):
    out = np.empty((2, 16384, 1024), np.float32)
    for core in range(8):
        b, c = core // 4, core % 4
        oc = results[core]["out"]
        for k in range(8):
            out[b, 512 * (4 * k + c):512 * (4 * k + c) + 512] = \
                oc[512 * k:512 * (k + 1)]
    return out


def kernel(**inputs):
    nc = _get_nc(STAGE)
    in_maps = make_in_maps(inputs)
    res = run_bass_kernel_spmd(nc, in_maps, core_ids=list(range(8)))
    return assemble_out(res.results)


# version 04
# speedup vs baseline: 1.0122x; 1.0122x over previous
import numpy as np
import ml_dtypes
from contextlib import ExitStack

import concourse.bass as bass
import concourse.mybir as mybir
from concourse.bass_utils import run_bass_kernel_spmd

F32 = mybir.dt.float32
BF16 = mybir.dt.bfloat16
AF = mybir.ActivationFunctionType
ALU = mybir.AluOpType
AX = mybir.AxisListType

D = 1024
KC = 8
DFF = 4096
NEXP = 8
TOK = 4096
NSLOT = 8
SLOT = 512
NGRP = 2
GRP = 2048
NTT = 4
TT = 512
NTS = 16
NPIECE = 8
HCP = 4
NEG = -30000.0
LN_EPS = 1e-5

STAGE = 3
SKIPSTEPS = False
DBG_NO_MASKS = False
DBG_NO_QX1 = False
DBG_NO_WO = False
DBG_NO_XVDUMP = False


class Ctr:
    __slots__ = ("h", "n", "name")

    def __init__(self, h, name):
        self.h = h
        self.n = 0
        self.name = name


class St:
    __slots__ = ("ops", "name", "seen", "ctr", "inck", "skip_self")

    def __init__(self, name, ctr, inck=1, skip_self=True):
        self.ops = []
        self.name = name
        self.seen = {}
        self.ctr = ctr
        self.inck = inck
        self.skip_self = skip_self

    def op(self, fn, waits=(), sw=False):
        ws = []
        if sw:
            ws.append((self.ctr, self.ctr.n))
        for c, v in waits:
            if c is None:
                continue
            if self.skip_self and c is self.ctr:
                continue
            assert v <= c.n, (
                f"[{self.name}] forward wait on {c.name}: need {v} > planned {c.n}"
            )
            if self.seen.get(c.name, -1) < v:
                ws.append((c, v))
                self.seen[c.name] = v
        self.ctr.n += self.inck

        def f(eng, fn=fn, ws=tuple(ws), sem=self.ctr.h, inck=self.inck):
            for c, v in ws:
                eng.wait_ge(c.h, v)
            ins = fn(eng)
            ins.then_inc(sem, inck)

        self.ops.append(f)
        return self.ctr.n


def build_nc(stage=STAGE):
    nc = bass.Bass("TRN2", target_bir_lowering=False, debug=False)

    x_d = nc.dram_tensor("x", [TOK, D], F32, kind="ExternalInput").ap()
    moff_d = nc.dram_tensor("moff", [128, 4], F32, kind="ExternalInput").ap()
    wq_d = nc.dram_tensor("wq", [128, KC * D], F32, kind="ExternalInput").ap()
    wk_d = nc.dram_tensor("wk", [128, KC * D], F32, kind="ExternalInput").ap()
    wv_d = nc.dram_tensor("wv", [128, KC * D], F32, kind="ExternalInput").ap()
    wo_d = nc.dram_tensor("wo", [128, KC * D], F32, kind="ExternalInput").ap()
    wr_d = nc.dram_tensor("wr", [128, KC * 7], F32, kind="ExternalInput").ap()
    wg_d = nc.dram_tensor("wg", [NEXP, NPIECE, 2, 128, 2048], F32, kind="ExternalInput").ap()
    wu_d = nc.dram_tensor("wu", [NEXP, NPIECE, 2, 128, 2048], F32, kind="ExternalInput").ap()
    wd_d = nc.dram_tensor("wd", [NEXP, NPIECE, 2, 128, 2048], F32, kind="ExternalInput").ap()
    out_d = nc.dram_tensor("out", [TOK, D], F32, kind="ExternalOutput").ap()

    x1sp = nc.dram_tensor("x1sp", [TOK, D], F32).ap()
    x2sp = nc.dram_tensor("x2sp", [TOK, D], F32).ap()
    qsp = nc.dram_tensor("qsp", [D, TOK], BF16).ap()
    kb_t = [nc.dram_tensor(f"kb{g}", [D, GRP], BF16) for g in range(NGRP)]
    vb_t = [nc.dram_tensor(f"vb{g}", [GRP, D], BF16) for g in range(NGRP)]
    kg_t = [nc.dram_tensor(f"kg{g}", [4, D, GRP], BF16) for g in range(NGRP)]
    vg_t = [nc.dram_tensor(f"vg{g}", [4, GRP, D], BF16) for g in range(NGRP)]
    db_t = nc.dram_tensor("dbr", [128, 128], BF16)
    dg_t = nc.dram_tensor("dgr", [4, 128, 128], BF16)
    kg = [t.ap() for t in kg_t]
    vg = [t.ap() for t in vg_t]

    with ExitStack() as ctx:
        def sb(name, shape, dt):
            return ctx.enter_context(nc.sbuf_tensor(name, shape, dt))

        def mksem(name):
            return Ctr(ctx.enter_context(nc.semaphore(name)), name)

        identb = sb("identb", [128, 128], BF16)
        identf = sb("identf", [128, 128], F32)
        wrsb = sb("wrsb", [128, KC * 7], F32)
        moffsb = sb("moffsb", [128, 4], F32)
        qscale = sb("qscale", [128, 1], F32)
        vdbg = sb("vdbg", [128, 4096], BF16) if stage == 4 else None
        AFa = sb("AFa", [128, 19840], F32)
        ABa = sb("ABa", [128, 45056], BF16)
        psf = [ctx.enter_context(nc.psum_tensor(f"psf{i}", [128, 512], F32))
               for i in range(8)]

        c_pe = mksem("c_pe")
        c_v = mksem("c_v")
        c_s = mksem("c_s")
        c_sy = mksem("c_sy")
        c_gp = mksem("c_gp")
        c_gpc = mksem("c_gpc")

        PE = St("pe", c_pe)
        VE = St("v", c_v)
        SE = St("s", c_s)
        SY = St("sy", c_sy, inck=16, skip_self=False)
        GPd = St("gp", c_gp, inck=16, skip_self=False)
        GPc = St("gpc", c_gpc)
        gp_ops = []
        GPd.ops = gp_ops
        GPc.ops = gp_ops
        GPc.seen = GPd.seen

        ACC = 0
        XSQ = 16384
        X32 = 17408
        X32T = 18432
        STAT = 19456
        LOGI = 19584
        CMB = 19696
        CTX = 0
        AX1 = 4096
        AX2 = 8192
        SSB = 12288
        IOTA = 13312
        ASM = 15360
        A_NM, A_L, A_NMN, A_DAL, A_ALP, A_ASUM, A_RL = (ASM + i * 4 for i in range(7))
        PBF = [15424, 15936, 16448, 16960]

        XLN = 0
        PB = 16384
        RING = [PB + i * 6144 for i in range(3)]
        HBUF = PB + 18432
        GSB = PB + 26624
        USB = PB + 27648
        XBF_FFN = PB + 26624
        WSL = [PB + 0, PB + 8192]
        KST = [PB + 16384, PB + 18432]
        VST = [PB + 20480, PB + 22528]
        XBF_QKV = PB + 24576
        QSB = 0
        ARING = [4096, 12288]
        MSK = 20480
        WOT = 28672
        PTB = [36864, 37376, 37888, 38400]
        CTXB = 38912
        CTXT = [43008, 44032]

        AFp = AFa.ap()
        ABp = ABa.ap()

        def af(o, n):
            return AFp[:, o:o + n]

        def ab(o, n):
            return ABp[:, o:o + n]

        bank_mark = {}
        state = {}

        def pe_group(fn, banks, waits=()):
            w = list(waits)
            for b in banks:
                m = bank_mark.get(b)
                if m:
                    w.append(m)
            return PE.op(fn, waits=w)

        def consume(banks, ctr, val):
            for b in banks:
                bank_mark[b] = (ctr, val)

        def barrier_wait():
            return [(c_pe, c_pe.n), (c_v, c_v.n), (c_s, c_s.n),
                    (c_sy, c_sy.n), (c_gp, c_gp.n), (c_gpc, c_gpc.n)]

        SY.op(lambda e: e.dma_start(out=moffsb[:, :], in_=moff_d))
        GPd.op(lambda e: e.dma_start(out=wrsb[:, :], in_=wr_d))
        GPc.op(lambda e: e.memset(identb[:, :], 0.0))
        GPc.op(lambda e: e.affine_select(
            out=identb[:, :], in_=identb[:, :], compare_op=ALU.not_equal,
            fill=1.0, base=0, pattern=[[-1, 128]], channel_multiplier=1))
        GPc.op(lambda e: e.memset(identf[:, :], 0.0))
        GPc.op(lambda e: e.affine_select(
            out=identf[:, :], in_=identf[:, :], compare_op=ALU.not_equal,
            fill=1.0, base=0, pattern=[[-1, 128]], channel_multiplier=1))
        GPc.op(lambda e: e.memset(qscale[:, :], 1.0 / 32.0))
        init_gpc = (c_gpc, c_gpc.n)
        init_sy = (c_sy, c_sy.n)

        def ln_group(site, grp_waits):
            router = site != "qkv"
            xbf_base = XBF_QKV if site == "qkv" else XBF_FFN
            ssum = af(STAT + 0, 16)
            ssq = af(STAT + 16, 16)
            rstd = af(STAT + 32, 16)
            nb = af(STAT + 48, 16)
            t1s = af(STAT + 64, 16)
            t2s = af(STAT + 80, 16)

            for ts in range(NTS):
                acct = af(ACC + ts * D, D)
                VE.op(lambda e, a=acct, o=af(STAT + ts, 1): e.reduce_sum(o, a, AX.X),
                      waits=grp_waits if ts == 0 else ())
                SE.op(lambda e, a=acct, o=af(STAT + 16 + ts, 1): e.activation(
                    af(XSQ, D), a, AF.Square, accum_out=o),
                    waits=grp_waits if ts == 0 else ())
            sqr_done = (c_s, c_s.n)
            sd_t = af(STAT + 96, 16)
            VE.op(lambda e: e.tensor_scalar_mul(t1s, ssum, 1.0 / D),
                  waits=[sqr_done], sw=True)
            VE.op(lambda e: e.tensor_mul(t2s, t1s, t1s), sw=True)
            VE.op(lambda e: e.scalar_tensor_tensor(t2s, ssq, 1.0 / D, t2s,
                                                   ALU.mult, ALU.subtract), sw=True)
            vv = VE.op(lambda e: e.tensor_scalar_add(t2s, t2s, LN_EPS), sw=True)
            SE.op(lambda e: e.activation(sd_t, t2s, AF.Sqrt),
                  waits=[(c_v, vv)])
            sv = c_s.n
            VE.op(lambda e: e.reciprocal(rstd, sd_t), waits=[(c_s, sv)])
            VE.op(lambda e: e.scalar_tensor_tensor(nb, t1s, -1.0, rstd,
                                                   ALU.mult, ALU.mult), sw=True)
            vstat = c_v.n

            for ts in range(NTS):
                acct = af(ACC + ts * D, D)
                par = ts % 2
                x32 = af(XSQ if par else X32, D)
                w = [(c_v, vstat), sqr_done]
                pk = state.get(("x32", par))
                if pk:
                    w.append(pk)
                SE.op(lambda e, a=acct, o=x32, ts=ts: e.activation(
                    o, a, AF.Identity, bias=nb[:, ts:ts + 1],
                    scale=rstd[:, ts:ts + 1]), waits=w)
                sB = c_s.n
                fT = (0, 1) if par == 0 else (2, 3)

                def f32T(e, x32=x32, fT=fT):
                    for fc in range(KC):
                        ins = e.transpose(
                            psf[fT[fc // 4]][:, (fc % 4) * 128:(fc % 4) * 128 + 128],
                            x32[:, fc * 128:fc * 128 + 128], identf[:, :])
                    return ins
                peF = pe_group(f32T, fT, waits=[(c_s, sB), init_gpc])
                state[("x32", par)] = (c_pe, peF)
                xo = ABp[:, XLN:XLN + KC * GRP].rearrange(
                    "p (fc t) -> p fc t", fc=KC)[:, :, ts * 128:(ts + 1) * 128]

                def ev(e, fT=fT, xo=xo):
                    e.tensor_copy(xo[:, 0:4, :],
                                  psf[fT[0]].ap().rearrange("p (a b) -> p a b", a=4))
                    return e.tensor_copy(xo[:, 4:8, :],
                                         psf[fT[1]].ap().rearrange("p (a b) -> p a b", a=4))
                evm = VE.op(ev, waits=[(c_pe, peF)])
                markers = [(c_v, evm)]

                if router:
                    x32t = af(X32T, D)
                    w = []
                    pk = state.get("x32t")
                    if pk:
                        w.append(pk)

                    def evf(e, fT=fT, x32t=x32t):
                        e.tensor_copy(x32t[:, 0:512], psf[fT[0]].ap())
                        return e.tensor_copy(x32t[:, 512:1024], psf[fT[1]].ap())
                    evfm = VE.op(evf, waits=w)
                    markers.append((c_v, evfm))
                    lbank = 4 if par == 0 else 5

                    def rmm(e, x32t=x32t, lbank=lbank):
                        for kc in range(KC):
                            ins = e.matmul(psf[lbank][:, 0:7],
                                           x32t[:, kc * 128:kc * 128 + 128],
                                           wrsb[:, kc * 7:kc * 7 + 7],
                                           start=(kc == 0), stop=(kc == KC - 1))
                        return ins
                    peR = pe_group(rmm, (lbank,), waits=[(c_v, evfm)])
                    state["x32t"] = (c_pe, peR)
                    lgt = af(LOGI + ts * 7, 7)
                    evl = VE.op(lambda e, lgt=lgt, lbank=lbank:
                                e.tensor_copy(lgt, psf[lbank][:, 0:7]),
                                waits=[(c_pe, peR)])
                    consume((lbank,), c_v, evl)
                consume(fT, *markers[-1])

            if router:
                e_t = af(X32 + 0, 7)
                em_t = af(X32 + 8, 7)
                tt1 = af(X32 + 16, 7)
                tt2 = af(X32 + 24, 7)
                m1 = af(X32 + 32, 1)
                e1 = af(X32 + 33, 1)
                e2 = af(X32 + 34, 1)
                sS = af(X32 + 35, 1)
                den = af(X32 + 36, 1)
                rcp = af(X32 + 37, 1)
                wfirst = [state[("x32", 0)], state[("x32", 1)]]
                for ts in range(NTS):
                    lgt = af(LOGI + ts * 7, 7)
                    VE.op(lambda e, lgt=lgt: e.reduce_max(m1, lgt, AX.X, negate=True),
                          waits=wfirst if ts == 0 else ())
                    vm = c_v.n
                    SE.op(lambda e, lgt=lgt: e.activation(e_t, lgt, AF.Exp,
                                                          bias=m1, accum_out=sS),
                          waits=[(c_v, vm)] + (wfirst if ts == 0 else []))
                    sm = c_s.n
                    VE.op(lambda e: e.reduce_max(e1, e_t, AX.X), waits=[(c_s, sm)])
                    VE.op(lambda e: e.tensor_single_scalar(tt1, e_t, e1, ALU.is_ge),
                          sw=True)
                    VE.op(lambda e: e.tensor_scalar(em_t, tt1, -1.0, 1.0,
                                                    ALU.mult, ALU.add), sw=True)
                    VE.op(lambda e: e.tensor_mul(em_t, em_t, e_t), sw=True)
                    VE.op(lambda e: e.reduce_max(e2, em_t, AX.X), sw=True)
                    VE.op(lambda e: e.tensor_single_scalar(tt2, em_t, e2, ALU.is_ge),
                          sw=True)
                    VE.op(lambda e: e.scalar_tensor_tensor(den, sS, 1e-9, e1,
                                                           ALU.mult, ALU.add))
                    VE.op(lambda e: e.tensor_add(den, den, e2), sw=True)
                    VE.op(lambda e: e.reciprocal(rcp, den), sw=True)
                    VE.op(lambda e: e.tensor_add(tt1, tt1, tt2))
                    VE.op(lambda e: e.tensor_mul(tt1, tt1, e_t), sw=True)
                    cmb = af(CMB + ts * 8 + 1, 7)
                    VE.op(lambda e, cmb=cmb: e.tensor_scalar(cmb, tt1, rcp, 0.5,
                                                             ALU.mult, ALU.mult),
                          sw=True)

        def ffn_experts(bw):
            xln_ready = (c_v, c_v.n)
            for e_i in range(NEXP):
                for p_i in range(NPIECE):
                    uds = []
                    for u_i in range(2):
                        rs = state.get("ring_rr", 0)
                        state["ring_rr"] = (rs + 1) % 3
                        slot = RING[rs]
                        w = list(bw)
                        fm = state.get(("ring_free", rs))
                        if fm:
                            w.append(fm)
                        for nm_, dsrc in (("g", wg_d), ("u", wu_d), ("d", wd_d)):
                            off = slot + {"g": 0, "u": 2048, "d": 4096}[nm_]
                            GPd.op(lambda e, d=dsrc[e_i, p_i, u_i], o=ab(off, 2048):
                                   e.dma_start(out=o, in_=d), waits=w)
                            w = []
                        uds.append((rs, c_gp.n))
                    w_h = state.get("hfree")
                    for hc in range(HCP):
                        rs, gval = uds[hc // 2]
                        slot = RING[rs]
                        dl = hc % 2
                        for tt in range(NTT):
                            uc = state.get("ffn_unit", 0)
                            state["ffn_unit"] = uc + 1
                            gb, ub = uc % 2, 2 + uc % 2

                            def gu(e, slot=slot, dl=dl, tt=tt, gb=gb, ub=ub):
                                for kc in range(KC):
                                    e.matmul(psf[gb][:, :],
                                             ab(slot + kc * 256 + dl * 128, 128),
                                             ab(XLN + kc * GRP + tt * TT, TT),
                                             start=(kc == 0), stop=(kc == KC - 1))
                                for kc in range(KC):
                                    ins = e.matmul(psf[ub][:, :],
                                                   ab(slot + 2048 + kc * 256 + dl * 128, 128),
                                                   ab(XLN + kc * GRP + tt * TT, TT),
                                                   start=(kc == 0), stop=(kc == KC - 1))
                                return ins
                            pem = pe_group(gu, (gb, ub),
                                           waits=[(c_gp, gval), xln_ready])
                            gsb = ab(GSB + (uc % 2) * 512, 512)
                            usb = ab(USB + (uc % 2) * 512, 512)
                            w = []
                            pk = state.get(("gub", uc % 2))
                            if pk:
                                w.append(pk)
                            SE.op(lambda e, gb=gb, o=gsb: e.activation(
                                o, psf[gb][:, :], AF.Silu), waits=[(c_pe, pem)] + w)
                            SE.op(lambda e, ub=ub, o=usb: e.copy(o, psf[ub][:, :]))
                            sm = c_s.n
                            consume((gb, ub), c_s, sm)
                            hdst = ab(HBUF + hc * GRP + tt * TT, TT)
                            w = [(c_s, sm)]
                            if w_h and hc == 0 and tt == 0:
                                w.append(w_h)
                            vm = VE.op(lambda e, g=gsb, u=usb, o=hdst:
                                       e.tensor_mul(o, g, u), waits=w)
                            state[("gub", uc % 2)] = (c_v, vm)
                    h_done = (c_v, c_v.n)
                    for ts in range(NTS):
                        pa, pb2 = 4 + 2 * (ts % 2), 5 + 2 * (ts % 2)

                        def dn(e, ts=ts, pa=pa, pb2=pb2, uds=uds):
                            for dffc in range(4):
                                slot = RING[uds[dffc // 2][0]]
                                lhs = ab(HBUF + dffc * GRP + ts * 128, 128)
                                e.matmul(psf[pa][:, :], lhs,
                                         ab(slot + 4096 + (dffc % 2) * 1024, 512),
                                         start=(dffc == 0), stop=(dffc == 3))
                                ins = e.matmul(psf[pb2][:, :], lhs,
                                               ab(slot + 4096 + (dffc % 2) * 1024 + 512, 512),
                                               start=(dffc == 0), stop=(dffc == 3))
                            return ins
                        pem = pe_group(dn, (pa, pb2), waits=[h_done])
                        if e_i == 0:
                            sc = 0.5
                        else:
                            sc = af(CMB + ts * 8 + e_i, 1)
                        VE.op(lambda e, ts=ts, pa=pa, s=sc: e.scalar_tensor_tensor(
                            af(ACC + ts * D, 512), psf[pa][:, :], s,
                            af(ACC + ts * D, 512), ALU.mult, ALU.add),
                            waits=[(c_pe, pem)])
                        vm = VE.op(lambda e, ts=ts, pb2=pb2, s=sc: e.scalar_tensor_tensor(
                            af(ACC + ts * D + 512, 512), psf[pb2][:, :], s,
                            af(ACC + ts * D + 512, 512), ALU.mult, ALU.add))
                        consume((pa, pb2), c_v, vm)
                    pe_end = (c_pe, c_pe.n)
                    for rs, _ in uds:
                        state[("ring_free", rs)] = pe_end
                    state["hfree"] = pe_end

        def qkv_group(gi):
            pe_exp_end = (c_pe, c_pe.n)
            acc_final = state["acc_final"]
            ln_group("qkv", [acc_final])
            gq = GPd.op(lambda e: e.dma_start(out=ab(WSL[0], 8192), in_=wq_d),
                        waits=[pe_exp_end])
            gk = GPd.op(lambda e: e.dma_start(out=ab(WSL[1], 8192), in_=wk_d))
            xln_ready = (c_v, c_v.n)

            def proj_fm(wslot, gw, dst, scale):
                for dc in range(KC):
                    par = dc % 2
                    stg = KST[par]
                    last = None
                    for tt in range(NTT):
                        bk = (dc * NTT + tt) % 2

                        def pj(e, wslot=wslot, dc=dc, tt=tt, bk=bk):
                            for kc in range(KC):
                                ins = e.matmul(psf[bk][:, :],
                                               ab(wslot + kc * D + dc * 128, 128),
                                               ab(XLN + kc * GRP + tt * TT, TT),
                                               start=(kc == 0), stop=(kc == KC - 1))
                            return ins
                        pem = pe_group(pj, (bk,), waits=[(c_gp, gw), xln_ready])
                        dm = state.get(("stg", par))
                        w2 = [(c_pe, pem)] + ([dm] if dm else [])
                        if scale is None:
                            SE.op(lambda e, bk=bk, o=ab(stg + tt * TT, TT):
                                  e.copy(o, psf[bk][:, :]), waits=w2)
                        else:
                            SE.op(lambda e, bk=bk, o=ab(stg + tt * TT, TT):
                                  e.mul(o, psf[bk][:, :], qscale[:, :]),
                                  waits=w2 + [init_gpc])
                        consume((bk,), c_s, c_s.n)
                        last = c_s.n
                    c0 = 0 if dst.shape[1] == GRP else gi * GRP
                    dmam = SY.op(lambda e, stg=stg, dc=dc, dst=dst, c0=c0:
                                 e.dma_start(out=dst[dc * 128:dc * 128 + 128,
                                                     c0:c0 + GRP],
                                             in_=ab(stg, GRP)),
                                 waits=[(c_s, last)])
                    state[("stg", par)] = (c_sy, dmam)

            proj_fm(WSL[0], gq, qsp, 1.0 / 32.0)
            gv = GPd.op(lambda e: e.dma_start(out=ab(WSL[0], 8192), in_=wv_d),
                        waits=[(c_pe, c_pe.n)])
            if gi == 1:
                emit_gathers(0)
            proj_fm(WSL[1], gk, kb_t[gi].ap(), None)

            for ts in range(NTS):
                par = (ts // 2) % 2
                pa, pb2 = 2 + 2 * (ts % 2), 3 + 2 * (ts % 2)

                def vp(e, ts=ts, pa=pa, pb2=pb2):
                    for kc in range(KC):
                        lhs = ab(XLN + kc * GRP + ts * 128, 128)
                        e.matmul(psf[pa][:, :], lhs, ab(WSL[0] + kc * D, 512),
                                 start=(kc == 0), stop=(kc == KC - 1))
                        ins = e.matmul(psf[pb2][:, :], lhs,
                                       ab(WSL[0] + kc * D + 512, 512),
                                       start=(kc == 0), stop=(kc == KC - 1))
                    return ins
                pem = pe_group(vp, (pa, pb2), waits=[(c_gp, gv), xln_ready])
                stg = VST[par] + (ts % 2) * 1024
                dm = state.get(("stgv", par))
                w2 = [(c_pe, pem)] + ([dm] if dm else [])
                SE.op(lambda e, pa=pa, o=ab(stg, 512): e.copy(o, psf[pa][:, :]),
                      waits=w2)
                SE.op(lambda e, pb2=pb2, o=ab(stg + 512, 512): e.copy(o, psf[pb2][:, :]))
                consume((pa, pb2), c_s, c_s.n)
                if ts % 2 == 1:
                    r0 = (ts - 1) * 128
                    vout = vb_t[gi].ap()[r0:r0 + 256, :].rearrange("(a p) f -> p a f", p=128)
                    vin = ABp[:, VST[par]:VST[par] + 2048].rearrange(
                        "p (a f) -> p a f", a=2)
                    dmam = SY.op(lambda e, vout=vout, vin=vin:
                                 e.dma_start(out=vout, in_=vin),
                                 waits=[(c_s, c_s.n)])
                    state[("stgv", par)] = (c_sy, dmam)

            xo = x1sp[gi * GRP:(gi + 1) * GRP, :].rearrange("(a p) f -> p a f", p=128)
            xi = AFp[:, ACC:ACC + NTS * D].rearrange("p (a f) -> p a f", a=NTS)
            SY.op(lambda e, xo=xo, xi=xi: e.dma_start(out=xo, in_=xi),
                  waits=[acc_final])

        def ffn_call(site, gi, src):
            bw = barrier_wait()
            xi = src[gi * GRP:(gi + 1) * GRP, :].rearrange("(a p) f -> p a f", p=128)
            xo = AFp[:, ACC:ACC + NTS * D].rearrange("p (a f) -> p a f", a=NTS)
            ld = SY.op(lambda e, xi=xi, xo=xo: e.dma_start(out=xo, in_=xi),
                       waits=bw)
            ln_group(site, [(c_sy, ld), init_sy] + bw)
            ffn_experts(bw)
            state["acc_final"] = (c_v, c_v.n)

        if stage == 0:
            bw = barrier_wait()
            xi = x_d[0:GRP, :].rearrange("(a p) f -> p a f", p=128)
            xo = AFp[:, ACC:ACC + NTS * D].rearrange("p (a f) -> p a f", a=NTS)
            ld = SY.op(lambda e, xi=xi, xo=xo: e.dma_start(out=xo, in_=xi), waits=bw)
            ln_group("ffn1", [(c_sy, ld), init_sy] + bw)
            lnv = (c_v, c_v.n)
            lnp = (c_pe, c_pe.n)
            xv = ABp[:, XLN:XLN + KC * GRP].rearrange("p (fc t) -> p fc t", fc=KC)
            for hh in range(2):
                ov = out_d[hh * 1024:(hh + 1) * 1024, :].rearrange(
                    "(fc p) t -> p fc t", p=128)
                GPd.op(lambda e, ov=ov, hh=hh: e.dma_start(
                    out=ov, in_=xv[:, :, hh * 1024:(hh + 1) * 1024]),
                    waits=[lnv, lnp])
            SY.op(lambda e: e.dma_start(out=out_d[2048:2176, 0:128], in_=af(CMB, 128)),
                  waits=[lnv])
            SY.op(lambda e: e.dma_start(out=out_d[2176:2304, 0:112], in_=af(LOGI, 112)),
                  waits=[lnv])
            SY.op(lambda e: e.dma_start(out=out_d[2304:2432, 0:128], in_=af(STAT, 128)),
                  waits=[lnv])

        cc_grp = [None, None]
        bounce_mark = [None, None]

        def emit_gathers(g):
            GPc.op(lambda e, g=g: e.collective_compute(
                "AllGather", ALU.bypass,
                replica_groups=[[0, 1, 2, 3], [4, 5, 6, 7]],
                ins=[kb_t[g].ap().opt()], outs=[kg_t[g].ap().opt()]),
                waits=[bounce_mark[g]])
            GPc.op(lambda e, g=g: e.collective_compute(
                "AllGather", ALU.bypass,
                replica_groups=[[0, 1, 2, 3], [4, 5, 6, 7]],
                ins=[vb_t[g].ap().opt()], outs=[vg_t[g].ap().opt()]))
            GPc.op(lambda e: e.collective_compute(
                "AllGather", ALU.bypass,
                replica_groups=[[0, 1, 2, 3], [4, 5, 6, 7]],
                ins=[db_t.ap().opt()], outs=[dg_t.ap().opt()]))
            cc_grp[g] = (c_gpc, c_gpc.n)
        for gi in range(NGRP if stage > 0 else 0):
            ffn_call("ffn1", gi, x_d)
            if stage >= 2:
                qkv_group(gi)
                bounce_mark[gi] = (c_sy, c_sy.n)
            else:
                oo = out_d[gi * GRP:(gi + 1) * GRP, :].rearrange("(a p) f -> p a f", p=128)
                xi = AFp[:, ACC:ACC + NTS * D].rearrange("p (a f) -> p a f", a=NTS)
                SY.op(lambda e, oo=oo, xi=xi: e.dma_start(out=oo, in_=xi),
                      waits=[state["acc_final"]])

        if stage >= 2:
            bw = [(c_pe, c_pe.n), (c_v, c_v.n), (c_s, c_s.n),
                  (c_sy, c_sy.n), (c_gp, c_gp.n)]
            GPc.op(lambda e: e.iota(af(IOTA, 2048), pattern=[[1, 2048]], base=0,
                                    channel_multiplier=-1,
                                    allow_small_or_imprecise_dtypes=True),
                   waits=bw)
            io_m = (c_gpc, c_gpc.n)
            emit_wo_and_g1 = True
            for s_i in range(0 if DBG_NO_MASKS else 4):
                mt = ab(MSK + s_i * 2048, 2048)
                VE.op(lambda e, mt=mt, s_i=s_i: e.tensor_single_scalar(
                    mt, af(IOTA, 2048), moffsb[:, s_i:s_i + 1], ALU.is_gt),
                    waits=[io_m, init_sy] + bw)
                VE.op(lambda e, mt=mt: e.tensor_scalar_mul(mt, mt, NEG), sw=True)
            msk_done = (c_v, c_v.n)
            gwo = None if DBG_NO_WO else GPd.op(lambda e: e.dma_start(out=ab(WOT, 8192), in_=wo_d), waits=bw)
            emit_gathers(1)

            nslot_eff = 1 if stage == 4 else NSLOT
            for s_i in range(nslot_eff):
                n_units = 1 if stage == 4 else 4 * (s_i + 1)
                qin = qsp.rearrange("(dc p) t -> p dc t", p=128)[
                    :, :, s_i * SLOT:(s_i + 1) * SLOT]
                w = list(bw)
                pm = state.get("qsb_free")
                if pm:
                    w.append(pm)
                qld = None if DBG_NO_QX1 else SY.op(lambda e, qin=qin: e.dma_start(
                    out=ABp[:, QSB:QSB + 4096].rearrange("p (dc t) -> p dc t", dc=KC),
                    in_=qin), waits=w)
                x1in = x1sp[s_i * SLOT:(s_i + 1) * SLOT, :].rearrange(
                    "(a p) f -> p a f", p=128)
                w = list(bw)
                pm = state.get("x1_free")
                if pm:
                    w.append(pm)
                x1ld = None if DBG_NO_QX1 else SY.op(lambda e, x1in=x1in: e.dma_start(
                    out=AFp[:, AX1:AX1 + 4096].rearrange("p (a f) -> p a f", a=4),
                    in_=x1in), waits=w)
                w = list(bw)
                pm = state.get("ctx_free")
                if pm:
                    w.append(pm)
                VE.op(lambda e: e.memset(af(CTX, 4096), 0.0), waits=w)
                w = []
                pm = state.get("nm_free")
                if pm:
                    w.append(pm)
                VE.op(lambda e: e.memset(af(A_NM, 4), 30000.0), waits=w)
                VE.op(lambda e: e.memset(af(A_L, 4), 0.0))
                st_init = (c_v, c_v.n)

                unit_dma = []

                def emit_kv_load(u):
                    rs = u % 2
                    g = u // 16
                    w = [cc_grp[g]] + list(bw)
                    fm = state.get(("aring_free", rs))
                    if fm:
                        w.append(fm)
                    klo = (u // 4) * SLOT - g * GRP
                    kin = kg[g][u % 4, :, klo:klo + SLOT]
                    kin = kin.rearrange("(dc p) t -> p dc t", p=128)
                    SY.op(lambda e, kin=kin, rs=rs: e.dma_start(
                        out=ABp[:, ARING[rs]:ARING[rs] + 4096].rearrange(
                            "p (dc t) -> p dc t", dc=KC), in_=kin), waits=w)
                    for tc in range(4):
                        vin = vg[g][u % 4, klo + tc * 128:klo + tc * 128 + 128, :]
                        dm = SY.op(lambda e, vin=vin, rs=rs, tc=tc: e.dma_start(
                            out=ab(ARING[rs] + 4096 + tc * 1024, 1024), in_=vin))
                    if stage == 4 and u == 0:
                        for tc in range(4):
                            vin = vg[0][0, tc * 128:tc * 128 + 128, :]
                            dm = SY.op(lambda e, vin=vin, tc=tc: e.dma_start(
                                out=vdbg.ap()[:, tc * 1024:(tc + 1) * 1024], in_=vin))
                    unit_dma.append(dm)

                for u in range(min(2, n_units)):
                    emit_kv_load(u)

                nqt = 0 if (stage == 4 and SKIPSTEPS) else 4

                def part2(sc_):
                    step, qt, rs, s_exp, pbuf, _u = sc_
                    alp = af(A_ALP + qt, 1)
                    ptp = 4 + step % 2

                    def ptr(e, p=pbuf, ptp=ptp):
                        for kc in range(4):
                            ins = e.transpose(psf[ptp][:, kc * 128:kc * 128 + 128],
                                              p[:, kc * 128:kc * 128 + 128],
                                              identf[:, :])
                        return ins
                    peT = pe_group(ptr, (ptp,), waits=[(c_s, s_exp)])
                    state[("pbuf", step % 4)] = (c_pe, peT)
                    ptb = ab(PTB[step % 4], 512)
                    w = [(c_pe, peT)]
                    pk = state.get(("ptbf", step % 4))
                    if pk:
                        w.append(pk)
                    SE.op(lambda e, ptp=ptp, o=ptb: e.copy(o, psf[ptp][:, :]),
                          waits=w)
                    s_pt = c_s.n
                    consume((ptp,), c_s, s_pt)

                    def av(e, ptb=ptb, rs=rs):
                        for kc in range(4):
                            e.matmul(psf[6][:, :], ptb[:, kc * 128:kc * 128 + 128],
                                     ab(ARING[rs] + 4096 + kc * D, 512),
                                     start=(kc == 0), stop=(kc == 3))
                            ins = e.matmul(psf[7][:, :],
                                           ptb[:, kc * 128:kc * 128 + 128],
                                           ab(ARING[rs] + 4096 + kc * D + 512, 512),
                                           start=(kc == 0), stop=(kc == 3))
                        return ins
                    peA = pe_group(av, (6, 7), waits=[(c_s, s_pt)])
                    state[("ptbf", step % 4)] = (c_pe, peA)
                    w = [(c_pe, peA), st_init]
                    VE.op(lambda e, qt=qt, alp=alp: e.scalar_tensor_tensor(
                        af(CTX + qt * D, 512), af(CTX + qt * D, 512), alp,
                        psf[6][:, :], ALU.mult, ALU.add), waits=w)
                    vm = VE.op(lambda e, qt=qt, alp=alp: e.scalar_tensor_tensor(
                        af(CTX + qt * D + 512, 512), af(CTX + qt * D + 512, 512),
                        alp, psf[7][:, :], ALU.mult, ALU.add))
                    consume((6, 7), c_v, vm)
                    state[("alw", qt)] = (c_v, vm)

                def unit_done(u):
                    state[("aring_free", u % 2)] = (c_pe, c_pe.n)
                    if u + 2 < n_units:
                        emit_kv_load(u + 2)

                step = 0
                pending = []
                steplist = [(u, qt) for u in range(n_units) for qt in range(nqt)]
                for u, qt in steplist:
                    rs = u % 2
                    diag = u >= 4 * s_i
                    if True:
                        spar = step % 4
                        nmv = af(A_NM + qt, 1)
                        nmn = af(A_NMN + qt, 1)
                        dal = af(A_DAL + qt, 1)
                        nmu = af(A_ASUM + qt, 1)
                        alp = af(A_ALP + qt, 1)
                        lv = af(A_L + qt, 1)

                        def sc(e, qt=qt, rs=rs, spar=spar):
                            for dc in range(KC):
                                ins = e.matmul(
                                    psf[spar][:, :],
                                    ABp[:, QSB + dc * SLOT + qt * 128:
                                        QSB + dc * SLOT + qt * 128 + 128],
                                    ab(ARING[rs] + dc * SLOT, SLOT),
                                    start=(dc == 0), stop=(dc == KC - 1))
                            return ins
                        pem = pe_group(sc, (spar,),
                                       waits=[(c_sy, unit_dma[u]), (c_sy, qld)])
                        if diag:
                            ssb = af(SSB + (step % 2) * 512, 512)
                            mslice = ab(MSK + qt * 2048 + (u - 4 * s_i) * 512, 512)
                            w = [(c_pe, pem), msk_done]
                            pk = state.get(("ssb", step % 2))
                            if pk:
                                w.append(pk)
                            vev = VE.op(lambda e, ssb=ssb, m=mslice, spar=spar:
                                        e.tensor_tensor(ssb, psf[spar][:, :], m, ALU.add),
                                        waits=w)
                            consume((spar,), c_v, vev)
                            VE.op(lambda e, ssb=ssb, nmu=nmu:
                                  e.reduce_max(nmu, ssb, AX.X, negate=True), sw=True)
                            exp_src = ssb
                        else:
                            w = [(c_pe, pem), st_init]
                            VE.op(lambda e, spar=spar, nmu=nmu:
                                  e.reduce_max(nmu, psf[spar][:, :], AX.X, negate=True),
                                  waits=w)
                            exp_src = psf[spar][:, :]
                        wnm = state.get(("nmw", qt))
                        VE.op(lambda e, nmn=nmn, nmv=nmv, nmu=nmu:
                              e.tensor_tensor(nmn, nmv, nmu, ALU.min),
                              waits=[wnm] if wnm else [], sw=True)
                        VE.op(lambda e, dal=dal, nmn=nmn, nmv=nmv:
                              e.tensor_sub(dal, nmn, nmv), sw=True)
                        VE.op(lambda e, nmv=nmv, nmn=nmn: e.tensor_copy(nmv, nmn))
                        vnm = c_v.n
                        wal = state.get(("alw", qt))
                        SE.op(lambda e, alp=alp, dal=dal: e.activation(alp, dal, AF.Exp),
                              waits=[(c_v, vnm)] + ([wal] if wal else []))
                        pbuf = af(PBF[step % 4], 512)
                        w = []
                        pk = state.get(("pbuf", step % 4))
                        if pk:
                            w.append(pk)
                        SE.op(lambda e, p=pbuf, src=exp_src, nmn=nmn, nmu=nmu:
                              e.activation(p, src, AF.Exp, bias=nmn, accum_out=nmu),
                              waits=w)
                        s_exp = c_s.n
                        state[("nmw", qt)] = (c_s, s_exp)
                        if diag:
                            state[("ssb", step % 2)] = (c_s, s_exp)
                        else:
                            consume((spar,), c_s, s_exp)
                        VE.op(lambda e, lv=lv, alp=alp, nmu=nmu:
                              e.scalar_tensor_tensor(lv, lv, alp, nmu,
                                                     ALU.mult, ALU.add),
                              waits=[(c_s, s_exp)])

                        cur = (step, qt, rs, s_exp, pbuf, u)
                        pending.append(cur)
                        if len(pending) > 2:
                            pd = pending.pop(0)
                            part2(pd)
                            if pd[1] == nqt - 1:
                                unit_done(pd[5])
                        step += 1
                for pd in pending:
                    part2(pd)
                    if pd[1] == nqt - 1:
                        unit_done(pd[5])
                pending = []
                state["qsb_free"] = (c_pe, c_pe.n)

                if stage == 4:
                    dw = [(c_v, c_v.n), (c_s, c_s.n), (c_pe, c_pe.n),
                          (c_sy, c_sy.n), (c_gp, c_gp.n)]
                    ov = out_d[0:512, :].rearrange("(a p) f -> p a f", p=128)
                    cv2 = AFp[:, CTX:CTX + 4096].rearrange("p (a f) -> p a f", a=4)
                    SY.op(lambda e: e.dma_start(out=ov, in_=cv2), waits=dw)
                    SY.op(lambda e: e.dma_start(out=out_d[512:640, 0:4],
                                                in_=af(A_L, 4)))
                    SY.op(lambda e: e.dma_start(out=out_d[640:768, 0:4],
                                                in_=af(A_NM, 4)))
                    qv = ABp[:, QSB:QSB + 4096].rearrange("p (a f) -> p a f", a=4)
                    GPd.op(lambda e: e.dma_start(
                        out=out_d[768:1280, :].rearrange("(a p) f -> p a f", p=128),
                        in_=qv), waits=dw)
                    GPd.op(lambda e: e.dma_start(out=out_d[1280:1408, :],
                                                 in_=ab(ARING[0], 1024)), waits=dw)
                    vcp = VE.op(lambda e: e.tensor_copy(af(IOTA, 1024),
                                                        ab(ARING[0] + 4096, 1024)),
                                waits=dw)
                    SY.op(lambda e: e.dma_start(out=out_d[1408:1536, :],
                                                in_=af(IOTA, 1024)),
                          waits=[(c_v, vcp)])
                    GPd.op(lambda e: e.dma_start(out=out_d[2304:3328, :],
                                                 in_=vg[0, 0:1024, :]))
                    GPd.op(lambda e: e.dma_start(out=out_d[3328:3456, :],
                                                 in_=ab(MSK, 1024)))
                    GPd.op(lambda e: e.dma_start(out=out_d[3456:3584, :],
                                                 in_=ab(MSK + 1024, 1024)))
                    SY.op(lambda e: e.dma_start(out=out_d[3584:3712, 0:512],
                                                in_=af(PBF[1], 512)), waits=dw)
                    GPd.op(lambda e: e.dma_start(out=out_d[3712:3840, 0:512],
                                                 in_=ab(PTB[1], 512)), waits=dw)

                    GPd.op(lambda e: e.dma_start(out=out_d[3840:3968, :],
                                                 in_=ab(ARING[0] + 4096, 1024)),
                           waits=dw)
                    GPd.op(lambda e: e.dma_start(out=out_d[3968:4096, :],
                                                 in_=ab(ARING[0] + 4096 + 1024, 1024)),
                           waits=dw)
                    SY.op(lambda e: e.dma_start(out=out_d[512:640, 8:12],
                                                in_=af(A_ALP, 4)))
                    SY.op(lambda e: e.dma_start(out=out_d[512:640, 12:16],
                                                in_=af(A_DAL, 4)))
                    GPd.op(lambda e: e.dma_start(out=out_d[1536:1664, :],
                                                 in_=vdbg.ap()[:, 0:1024]), waits=dw)
                    break

                for qt in range(4):
                    rl = af(A_RL + qt, 1)
                    VE.op(lambda e, rl=rl, qt=qt: e.reciprocal(rl, af(A_L + qt, 1)))
                    v_cb = VE.op(lambda e, qt=qt, rl=rl: e.tensor_scalar_mul(
                        af(CTX + qt * D, D), af(CTX + qt * D, D), rl), sw=True)
                    ctp = 4 + qt % 2
                    wpt = state.get(("cttw", qt % 2))
                    for half in range(2):
                        def cT2(e, qt=qt, ctp=ctp, half=half):
                            for fc in range(4):
                                ins = e.transpose(
                                    psf[ctp][:, fc * 128:fc * 128 + 128],
                                    af(CTX + qt * D + (half * 4 + fc) * 128, 128),
                                    identf[:, :])
                            return ins
                        peC = pe_group(cT2, (ctp,), waits=[(c_v, v_cb)])
                        w2 = [(c_pe, peC)]
                        if half == 0 and wpt:
                            w2.append(wpt)
                        SE.op(lambda e, ctp=ctp,
                              o=ab(CTXT[qt % 2] + half * 512, 512):
                              e.copy(o, psf[ctp][:, :]), waits=w2)
                        consume((ctp,), c_s, c_s.n)
                    s_ct = c_s.n

                    def op_(e, qt=qt):
                        for dc in range(KC):
                            lhs = ab(CTXT[qt % 2] + dc * 128, 128)
                            e.matmul(psf[6][:, :], lhs, ab(WOT + dc * D, 512),
                                     start=(dc == 0), stop=(dc == KC - 1))
                            ins = e.matmul(psf[7][:, :], lhs,
                                           ab(WOT + dc * D + 512, 512),
                                           start=(dc == 0), stop=(dc == KC - 1))
                        return ins
                    peO = pe_group(op_, (6, 7), waits=[(c_s, s_ct), (c_gp, gwo)])
                    state[("cttw", qt % 2)] = (c_pe, peO)
                    w = [(c_pe, peO), (c_sy, x1ld)]
                    pm = state.get("x2_free")
                    if pm:
                        w.append(pm)
                    VE.op(lambda e, qt=qt: e.tensor_add(
                        af(AX2 + qt * D, 512), af(AX1 + qt * D, 512), psf[6][:, :]),
                        waits=w)
                    vm = VE.op(lambda e, qt=qt: e.tensor_add(
                        af(AX2 + qt * D + 512, 512), af(AX1 + qt * D + 512, 512),
                        psf[7][:, :]))
                    consume((6, 7), c_v, vm)
                v_x2 = (c_v, c_v.n)
                state["x1_free"] = v_x2
                state["ctx_free"] = (c_pe, c_pe.n)
                state["nm_free"] = (c_s, c_s.n)
                xo2 = x2sp[s_i * SLOT:(s_i + 1) * SLOT, :].rearrange(
                    "(a p) f -> p a f", p=128)
                xi2 = AFp[:, AX2:AX2 + 4096].rearrange("p (a f) -> p a f", a=4)
                dmx = SY.op(lambda e, xo2=xo2, xi2=xi2: e.dma_start(out=xo2, in_=xi2),
                            waits=[v_x2])
                if stage == 2:
                    oo = out_d[s_i * SLOT:(s_i + 1) * SLOT, :].rearrange(
                        "(a p) f -> p a f", p=128)
                    dmx = SY.op(lambda e, oo=oo, xi2=xi2: e.dma_start(out=oo, in_=xi2))
                state["x2_free"] = (c_sy, dmx)

        if stage == 3:
            for gi in range(NGRP):
                ffn_call("ffn2", gi, x2sp)
                oo = out_d[gi * GRP:(gi + 1) * GRP, :].rearrange("(a p) f -> p a f", p=128)
                xi = AFp[:, ACC:ACC + NTS * D].rearrange("p (a f) -> p a f", a=NTS)
                SY.op(lambda e, oo=oo, xi=xi: e.dma_start(out=oo, in_=xi),
                      waits=[state["acc_final"]])

        fin = c_sy.n

        with nc.Block() as block:
            block.tensor(lambda t: ([f(t) for f in PE.ops], None)[1])
            block.vector(lambda v: ([f(v) for f in VE.ops], None)[1])
            block.scalar(lambda s: ([f(s) for f in SE.ops], None)[1])
            block.gpsimd(lambda g: ([f(g) for f in gp_ops], None)[1])

            def syrun(sy):
                for f in SY.ops:
                    f(sy)
                sy.wait_ge(c_sy.h, fin)
            block.sync(syrun)

    return nc


def _relayout_gateup(w):
    t = np.ascontiguousarray(w.T)
    t = t.reshape(8, 128, 8, 2, 256)
    t = t.transpose(2, 3, 1, 0, 4)
    return np.ascontiguousarray(t.reshape(8, 2, 128, 2048))


def _relayout_down(w):
    t = np.ascontiguousarray(w.T)
    t = t.reshape(8, 2, 2, 128, 1024)
    t = t.transpose(0, 1, 3, 2, 4)
    return np.ascontiguousarray(t.reshape(8, 2, 128, 2048))


def _relayout_sq(w):
    t = np.ascontiguousarray(w.T)
    t = t.reshape(8, 128, 1024).transpose(1, 0, 2)
    return np.ascontiguousarray(t.reshape(128, 8192))


_NC_CACHE = {}


def _get_nc(stage=STAGE):
    if stage not in _NC_CACHE:
        _NC_CACHE[stage] = build_nc(stage)
    return _NC_CACHE[stage]


def make_in_maps(inputs):
    f32 = np.float32
    x = np.asarray(inputs["L0_attn_in_features"], dtype=f32)
    wq = _relayout_sq(np.asarray(inputs["L0_attn_w_q"], dtype=f32))
    wk = _relayout_sq(np.asarray(inputs["L0_attn_w_k"], dtype=f32))
    wv = _relayout_sq(np.asarray(inputs["L0_attn_w_v"], dtype=f32))
    wo = _relayout_sq(np.asarray(inputs["L0_attn_w_o"], dtype=f32))
    wr = np.ascontiguousarray(
        np.asarray(inputs["L0_ffn_router_w"], dtype=f32).T.reshape(8, 128, 7)
        .transpose(1, 0, 2).reshape(128, 56))
    sg = np.asarray(inputs["L0_ffn_shared_gate"], dtype=f32)
    su = np.asarray(inputs["L0_ffn_shared_up"], dtype=f32)
    sd = np.asarray(inputs["L0_ffn_shared_down"], dtype=f32)
    rg = np.asarray(inputs["L0_ffn_routed_gate"], dtype=f32)
    ru = np.asarray(inputs["L0_ffn_routed_up"], dtype=f32)
    rd = np.asarray(inputs["L0_ffn_routed_down"], dtype=f32)
    wg = np.stack([_relayout_gateup(sg)] + [_relayout_gateup(rg[i]) for i in range(7)])
    wu = np.stack([_relayout_gateup(su)] + [_relayout_gateup(ru[i]) for i in range(7)])
    wd = np.stack([_relayout_down(sd)] + [_relayout_down(rd[i]) for i in range(7)])

    in_maps = []
    for core in range(8):
        b, c = core // 4, core % 4
        rows = np.concatenate(
            [x[b, 512 * (4 * k + c):512 * (4 * k + c) + 512] for k in range(8)], axis=0)
        moff = np.tile((512 * c + 128 * np.arange(4, dtype=f32))[None, :], (128, 1))
        in_maps.append({
            "x": np.ascontiguousarray(rows), "moff": np.ascontiguousarray(moff),
            "wq": wq, "wk": wk, "wv": wv, "wo": wo, "wr": wr,
            "wg": wg, "wu": wu, "wd": wd,
        })
    return in_maps


def assemble_out(results):
    out = np.empty((2, 16384, 1024), np.float32)
    for core in range(8):
        b, c = core // 4, core % 4
        oc = results[core]["out"]
        for k in range(8):
            out[b, 512 * (4 * k + c):512 * (4 * k + c) + 512] = \
                oc[512 * k:512 * (k + 1)]
    return out


def kernel(**inputs):
    nc = _get_nc(STAGE)
    in_maps = make_in_maps(inputs)
    res = run_bass_kernel_spmd(nc, in_maps, core_ids=list(range(8)))
    return assemble_out(res.results)
